# version 02
# baseline (speedup 1.0000x reference)
import math
import contextlib
import numpy as np
import concourse.bass as bass
import concourse.mybir as mybir
from concourse.bass_utils import run_bass_kernel_spmd

F32 = mybir.dt.float32
F16 = mybir.dt.float16
ALU = mybir.AluOpType
ACT = mybir.ActivationFunctionType
AX = mybir.AxisListType

f32 = np.float32
PI = math.pi
TWO_PI = 2.0 * math.pi

D = 1024
DC = D // 128
N_CORES = 8


def _host_consts():
    wl = np.arange(1, D + 1, dtype=np.float32) * f32(TWO_PI / D)
    inv_wl = (f32(1.0) / (wl + f32(1e-8))).astype(np.float32)
    c = inv_wl.astype(np.float64) / TWO_PI
    c_hi = c.astype(np.float32)
    c_lo = (c - c_hi.astype(np.float64)).astype(np.float32)
    QB, PBM = 48.0, 3.2
    m_folds = []
    for ec in range(DC):
        rr = (QB * float(inv_wl[ec * 128:(ec + 1) * 128].max()) + PBM) / TWO_PI
        m_folds.append(max(0, int(math.ceil(math.log2(max(rr, 1.0))))))
    return inv_wl, c_hi, c_lo, m_folds


INV_WL, C2PI_HI, C2PI_LO, M_FOLDS = _host_consts()


def _pp(v):
    return np.ascontiguousarray(v.reshape(DC, 128).T)


ENGS = ("sync", "vector", "scalar", "tensor", "gpsimd")


class Ctr:
    def __init__(self, es, nc, name):
        self.sem = es.enter_context(nc.semaphore(name))
        self.v = 0


class Plan:

    def __init__(self, es, nc, block):
        self.nc = nc
        self.es = es
        self.block = block
        self.steps = {e: [] for e in ENGS}
        self.c = {e: Ctr(es, nc, "c_" + e) for e in ENGS}
        self.all_ctrs = list(self.c.values())
        self.buf = {}
        self.dma_ctrs = {}
        self.last_wait = {e: {} for e in ENGS}

    def dma_ctr(self, buf):
        k = id(buf)
        if k not in self.dma_ctrs:
            ctr = Ctr(self.es, self.nc, "d%d" % len(self.dma_ctrs))
            self.dma_ctrs[k] = ctr
            self.all_ctrs.append(ctr)
        return self.dma_ctrs[k]

    def _dep(self, reads, writes):
        waits = {}
        for b in reads:
            st = self.buf.get(id(b))
            if st and st["ready"]:
                c, v = st["ready"]
                waits[c] = max(waits.get(c, 0), v)
        for b in writes:
            st = self.buf.get(id(b))
            if st:
                if st["ready"]:
                    c, v = st["ready"]
                    waits[c] = max(waits.get(c, 0), v)
                for c, v in st["cons"].items():
                    waits[c] = max(waits.get(c, 0), v)
        return waits

    def add(self, eng, emit, reads=(), writes=(), extra_waits=(), dma=0,
            nop=False, dma_buf=None):
        ctr = self.dma_ctr(dma_buf) if dma_buf is not None else self.c[eng]
        waits = self._dep(reads, writes)
        for (c2, v2) in extra_waits:
            waits[c2] = max(waits.get(c2, 0), v2)
        n = 16 * dma if dma else 1
        token = (ctr, ctr.v + n)
        self.steps[eng].append(
            {"emit": emit, "waits": waits, "ctr": ctr, "dma": dma, "nop": nop,
             "waitonly": False})
        ctr.v += n
        for b in reads:
            st = self.buf.setdefault(id(b), {"ready": None, "cons": {}})
            st["cons"][token[0]] = max(st["cons"].get(token[0], 0), token[1])
        for b in writes:
            self.buf[id(b)] = {"ready": token, "cons": {}}
        return token

    def barrier(self):
        vals = [(c, c.v) for c in self.all_ctrs]
        for eng in ENGS:
            ew = [(c, v) for c, v in vals if c is not self.c[eng]]
            waits = {}
            for (c2, v2) in ew:
                waits[c2] = max(waits.get(c2, 0), v2)
            self.steps[eng].append({"emit": None, "waits": waits,
                                    "ctr": self.c[eng], "dma": 0,
                                    "nop": False, "waitonly": True})

    def flush(self):
        for eng in ENGS:
            steps = self.steps[eng]
            if not steps:
                continue
            self.steps[eng] = []
            my = self.c[eng]
            lw = self.last_wait[eng]

            def section(engine, steps=steps, my=my, lw=lw):
                for st in steps:
                    for c, v in sorted(st["waits"].items(),
                                       key=lambda kv: id(kv[0])):
                        if lw.get(id(c), -1) < v:
                            engine.wait_ge(c.sem, v)
                            lw[id(c)] = v
                    if st["waitonly"]:
                        pass
                    elif st["nop"]:
                        engine.nop().then_inc(st["ctr"].sem, 1)
                    elif st["dma"]:
                        st["emit"](engine, st["ctr"].sem)
                    else:
                        st["emit"](engine).then_inc(st["ctr"].sem, 1)

            getattr(self.block, eng)(section)


def build_nc(SEQS=4, S=2048, QQ=512, TB=256, dbg=False):
    T = SEQS * S
    SB = QQ // 128
    NQQ = S // QQ
    NTB = S // TB
    TC = TB // 128
    assert SB % 2 == 0 and S % QQ == 0 and S % TB == 0
    nc = bass.Bass("TRN2", target_bir_lowering=False)

    x_d = nc.dram_tensor("x", [T, D], F32, kind="ExternalInput").ap()
    w_d = {p: nc.dram_tensor("w" + p, [D, D], F32, kind="ExternalInput").ap()
           for p in "qkv"}
    b_d = {p: nc.dram_tensor("b" + p, [1, D], F32, kind="ExternalInput").ap()
           for p in "qkv"}
    pb_pp_d = nc.dram_tensor("pb_pp", [128, DC], F32, kind="ExternalInput").ap()
    kern_d = nc.dram_tensor("kern", [D, D], F32, kind="ExternalInput").ap()
    fb_d = nc.dram_tensor("fb", [1, D], F32, kind="ExternalInput").ap()
    a_d = nc.dram_tensor("a", [D, 16], F32, kind="ExternalInput").ap()
    bl_d = nc.dram_tensor("bl", [16, D], F32, kind="ExternalInput").ap()
    chi_d = nc.dram_tensor("chi", [128, DC], F32, kind="ExternalInput").ap()
    clo_d = nc.dram_tensor("clo", [128, DC], F32, kind="ExternalInput").ap()
    id16_d = nc.dram_tensor("id16", [128, 128], F16, kind="ExternalInput").ap()
    idf_d = nc.dram_tensor("idf", [128, 128], F32, kind="ExternalInput").ap()
    out_d = nc.dram_tensor("out", [T, D], F32, kind="ExternalOutput").ap()

    def scr(name, shape, dt):
        return nc.dram_tensor(name, shape, dt,
                              kind="ExternalOutput" if dbg else "Internal").ap()

    xnh_d, xnl_d = scr("xnh", [D, T], F16), scr("xnl", [D, T], F16)
    feat_d = {nm: (scr(nm + "h", [D, T], F16), scr(nm + "l", [D, T], F16))
              for nm in ("cq", "sq", "ck", "sk")}
    vth_d, vtl_d = scr("vth", [T, D], F16), scr("vtl", [T, D], F16)
    attn_d = scr("attn", [T, D], F32)
    kqh_d, kql_d = scr("kqh", [D, D], F16), scr("kql", [D, D], F16)

    es = contextlib.ExitStack()
    with es:
        def sb(name, shape, dt):
            return es.enter_context(nc.sbuf_tensor(name, shape, dt))

        id16 = sb("s_id16", [128, 128], F16)
        idf = sb("s_idf", [128, 128], F32)
        chi = sb("s_chi", [128, DC], F32)
        clo = sb("s_clo", [128, DC], F32)
        pbz = sb("pbz", [128, DC], F32)
        ones16 = sb("ones16", [1, 1024], F16)
        onesf = sb("onesf", [1, 128], F32)
        brow = {p: (sb("b" + p + "hr", [1, D], F16),
                    sb("b" + p + "lr", [1, D], F16))
                for p in ("q", "k", "v", "f")}
        a16h = sb("a16h", [128, DC * 16], F16)
        a16l = sb("a16l", [128, DC * 16], F16)
        b16h = sb("b16h", [16, D], F16)
        b16l = sb("b16l", [16, D], F16)
        rowbuf = sb("rowbuf", [1, D], F32)

        block = es.enter_context(nc.Block())
        plan = Plan(es, nc, block)

        def dload(dst_ap, src_ap, reads=(), writes=()):
            assert len(writes) == 1
            def emit(engine, sem, dst=dst_ap, src=src_ap):
                engine.dma_start(out=dst, in_=src).then_inc(sem, 16)
            return plan.add("sync", emit, reads=reads, writes=writes, dma=1,
                            dma_buf=writes[0])

        def dstore(dst_ap, src_ap, reads=(), writes=()):
            assert len(reads) == 1
            def emit(engine, sem, dst=dst_ap, src=src_ap):
                engine.dma_start(out=dst, in_=src).then_inc(sem, 16)
            return plan.add("gpsimd", emit, reads=reads, writes=writes, dma=1,
                            dma_buf=reads[0])

        ve = lambda emit, **kw: plan.add("vector", emit, **kw)
        se = lambda emit, **kw: plan.add("scalar", emit, **kw)
        pe = lambda emit, **kw: plan.add("tensor", emit, **kw)

        dload(id16[:], id16_d, writes=[id16])
        dload(idf[:], idf_d, writes=[idf])
        dload(chi[:], chi_d, writes=[chi])
        dload(clo[:], clo_d, writes=[clo])
        dload(pbz[:], pb_pp_d, writes=[pbz])
        ve(lambda v: v.memset(ones16[:1, :], 1.0), writes=[ones16])
        ve(lambda v: v.memset(onesf[:1, :], 1.0), writes=[onesf])
        ve(lambda v: v.tensor_scalar(pbz[:], pbz[:], float(f32(1.0 / TWO_PI)), None,
                                     ALU.mult), reads=[pbz], writes=[pbz])

        for p, src in (("q", b_d["q"]), ("k", b_d["k"]), ("v", b_d["v"]),
                       ("f", fb_d)):
            dload(rowbuf[:1, :], src, writes=[rowbuf])
            h, l = brow[p]
            ve(lambda v, h=h: v.tensor_copy(h[:1, :], rowbuf[:1, :]),
               reads=[rowbuf], writes=[h])
            ve(lambda v, h=h, l=l: v.scalar_tensor_tensor(
                l[:1, :], h[:1, :], -1.0, rowbuf[:1, :], ALU.mult, ALU.add),
               reads=[rowbuf, h], writes=[l])

        with contextlib.ExitStack() as _es:
            abuf = _es.enter_context(nc.sbuf_tensor("abuf", [128, DC * 16], F32))
            bbuf = _es.enter_context(nc.sbuf_tensor("bbuf", [16, D], F32))
            for dc in range(DC):
                dload(abuf[:, dc * 16:(dc + 1) * 16],
                      a_d[dc * 128:(dc + 1) * 128, :], writes=[abuf])
            ve(lambda v: v.tensor_copy(a16h[:], abuf[:]), reads=[abuf],
               writes=[a16h])
            ve(lambda v: v.scalar_tensor_tensor(a16l[:], a16h[:], -1.0,
                                                abuf[:], ALU.mult, ALU.add),
               reads=[abuf, a16h], writes=[a16l])
            dload(bbuf[:16, :], bl_d, writes=[bbuf])
            ve(lambda v: v.tensor_copy(b16h[:16, :], bbuf[:16, :]),
               reads=[bbuf], writes=[b16h])
            ve(lambda v: v.scalar_tensor_tensor(b16l[:16, :], b16h[:16, :],
                                                -1.0, bbuf[:16, :], ALU.mult,
                                                ALU.add),
               reads=[bbuf, b16h], writes=[b16l])
            plan.barrier()
            plan.flush()

        with contextlib.ExitStack() as _es:
            kw = _es.enter_context(nc.sbuf_tensor("kw", [128, DC * 1024], F32))
            qs = _es.enter_context(nc.sbuf_tensor("qs", [128, 8], F32))
            qz = _es.enter_context(nc.sbuf_tensor("qz", [128, 1024], F32))
            qg = _es.enter_context(nc.sbuf_tensor("qg", [128, 1024], F32))
            qt = _es.enter_context(nc.sbuf_tensor("qt", [128, 1024], F32))
            k16a = _es.enter_context(nc.sbuf_tensor("k16a", [128, 1024], F16))
            k16b = _es.enter_context(nc.sbuf_tensor("k16b", [128, 1024], F16))
            qrow = _es.enter_context(nc.sbuf_tensor("qrow", [1, 128], F32))
            pq0 = _es.enter_context(nc.psum_tensor("pq0", [128, 512], F32))
            for dc in range(DC):
                dload(kw[:, dc * 1024:(dc + 1) * 1024],
                      kern_d[dc * 128:(dc + 1) * 128, :], writes=[kw])
            ve(lambda v: v.reduce_max(qs[:, 0:1], kw[:], axis=AX.X,
                                      apply_absolute_value=True),
               reads=[kw], writes=[qs])
            pe(lambda t: t.transpose(pq0[:1, 0:128], qs[:, 0:1], idf[:]),
               reads=[qs, idf], writes=[pq0])
            ve(lambda v: v.tensor_copy(qrow[:1, :], pq0[:1, 0:128]),
               reads=[pq0], writes=[qrow])
            ve(lambda v: v.reduce_max(qrow[:1, 0:1], qrow[:1, :], axis=AX.X),
               reads=[qrow], writes=[qrow])
            pe(lambda t: t.matmul(pq0[:, 0:1], onesf[:1, :], qrow[:1, 0:1],
                                  start=True, stop=True),
               reads=[qrow, onesf], writes=[pq0])
            ve(lambda v: v.tensor_copy(qs[:, 0:1], pq0[:, 0:1]),
               reads=[pq0], writes=[qs])
            c7h = float(f32(1.0 / 7.0))
            c7l = float(f32(np.float64(1.0 / 7.0) - np.float64(c7h)))
            ve(lambda v: v.tensor_scalar(qs[:, 1:2], qs[:, 0:1], c7h, None,
                                         ALU.mult), reads=[qs], writes=[qs])
            ve(lambda v: v.scalar_tensor_tensor(qs[:, 1:2], qs[:, 0:1], c7l,
                                                qs[:, 1:2], ALU.mult, ALU.add),
               reads=[qs], writes=[qs])
            ve(lambda v: v.reciprocal(qs[:, 2:3], qs[:, 0:1]),
               reads=[qs], writes=[qs])
            ve(lambda v: v.tensor_scalar(qs[:, 2:3], qs[:, 2:3], 7.0, None,
                                         ALU.mult), reads=[qs], writes=[qs])
            for dc in range(DC):
                sl = slice(dc * 1024, (dc + 1) * 1024)
                ve(lambda v, sl=sl: v.tensor_scalar(
                    qz[:], kw[:, sl], qs[:, 2:3], 8.5, ALU.mult, ALU.add),
                   reads=[kw, qs], writes=[qz])
                ve(lambda v: v.tensor_copy(qg[:], qz[:]), reads=[qz],
                   writes=[qg])
                for H in (8.0, 4.0, 2.0, 1.0):
                    ve(lambda v, H=H: v.tensor_scalar(
                        qt[:], qg[:], float(H), float(H), ALU.is_ge, ALU.mult),
                       reads=[qg], writes=[qt])
                    ve(lambda v: v.tensor_tensor(qg[:], qg[:], qt[:],
                                                 ALU.subtract),
                       reads=[qg, qt], writes=[qg])
                ve(lambda v: v.tensor_tensor(qz[:], qz[:], qg[:],
                                             ALU.subtract),
                   reads=[qz, qg], writes=[qz])
                ve(lambda v: v.tensor_scalar(qz[:], qz[:], 0.0, 15.0,
                                             ALU.max, ALU.min),
                   reads=[qz], writes=[qz])
                ve(lambda v: v.tensor_scalar(qz[:], qz[:], -8.0, qs[:, 1:2],
                                             ALU.add, ALU.mult),
                   reads=[qz, qs], writes=[qz])
                ve(lambda v: v.tensor_copy(k16a[:], qz[:]), reads=[qz],
                   writes=[k16a])
                ve(lambda v: v.scalar_tensor_tensor(k16b[:], k16a[:], -1.0,
                                                    qz[:], ALU.mult, ALU.add),
                   reads=[qz, k16a], writes=[k16b])
                dstore(kqh_d[dc * 128:(dc + 1) * 128, :], k16a[:],
                       reads=[k16a])
                dstore(kql_d[dc * 128:(dc + 1) * 128, :], k16b[:],
                       reads=[k16b])
            plan.barrier()
            plan.flush()

        with contextlib.ExitStack() as es2:
            wT = {p: (es2.enter_context(
                          nc.sbuf_tensor("wT" + p + "h", [128, DC * 1024], F16)),
                      es2.enter_context(
                          nc.sbuf_tensor("wT" + p + "l", [128, DC * 1024], F16)))
                  for p in "qkv"}
            with contextlib.ExitStack() as _es:
                wbuf = _es.enter_context(nc.sbuf_tensor("wbuf", [128, 1024], F32))
                wh16 = _es.enter_context(nc.sbuf_tensor("wh16", [128, 1024], F16))
                wl16 = _es.enter_context(nc.sbuf_tensor("wl16", [128, 1024], F16))
                ptr = _es.enter_context(nc.psum_tensor("ptr", [128, 1024], F16))
                for p in "qkv":
                    for ecw in range(DC):
                        dload(wbuf[:], w_d[p][ecw * 128:(ecw + 1) * 128, :],
                              writes=[wbuf])
                        ve(lambda v: v.tensor_copy(wh16[:], wbuf[:]),
                           reads=[wbuf], writes=[wh16])
                        ve(lambda v: v.scalar_tensor_tensor(
                            wl16[:], wh16[:], -1.0, wbuf[:], ALU.mult,
                            ALU.add), reads=[wbuf, wh16], writes=[wl16])
                        for half, wsrc in ((0, wh16), (1, wl16)):
                            dstw = wT[p][half]
                            for dc in range(DC):
                                pe(lambda t, wsrc=wsrc, dc=dc: t.transpose(
                                    ptr[:, 0:128],
                                    wsrc[:, dc * 128:(dc + 1) * 128], id16[:]),
                                   reads=[wsrc, id16], writes=[ptr])
                                ve(lambda v, dstw=dstw, dc=dc, ecw=ecw:
                                   v.tensor_copy(
                                       dstw[:, dc * 1024 + ecw * 128:
                                            dc * 1024 + (ecw + 1) * 128],
                                       ptr[:, 0:128]),
                                   reads=[ptr], writes=[dstw])
                plan.barrier()
                plan.flush()

            with contextlib.ExitStack() as _es:
                xt = _es.enter_context(nc.sbuf_tensor("xt", [128, 4 * 1024], F32))
                st1 = _es.enter_context(nc.sbuf_tensor("st1", [128, 8], F32))
                xc = _es.enter_context(nc.sbuf_tensor("xc", [128, 1024], F32))
                xsq = _es.enter_context(nc.sbuf_tensor("xsq", [128, 1024], F32))
                xnh = _es.enter_context(nc.sbuf_tensor("s_xnh", [128, 1024], F16))
                xnl = _es.enter_context(nc.sbuf_tensor("s_xnl", [128, 1024], F16))
                fmh = _es.enter_context(nc.sbuf_tensor("fmh", [128, DC * 512], F16))
                fml = _es.enter_context(nc.sbuf_tensor("fml", [128, DC * 512], F16))
                ptl = _es.enter_context(nc.psum_tensor("ptl", [128, 1024], F16))
                for sup in range(T // 512):
                    t0 = sup * 512
                    for i in range(4):
                        dload(xt[:, i * 1024:(i + 1) * 1024],
                              x_d[t0 + i * 128:t0 + (i + 1) * 128, :],
                              writes=[xt])
                    for i in range(4):
                        xsl = xt[:, i * 1024:(i + 1) * 1024]
                        ve(lambda v, xsl=xsl: v.reduce_sum(
                            st1[:, 0:1], xsl, axis=AX.X),
                           reads=[xt], writes=[st1])
                        ve(lambda v: v.tensor_scalar(
                            st1[:, 1:2], st1[:, 0:1], float(f32(1.0 / D)), None,
                            ALU.mult), reads=[st1], writes=[st1])
                        ve(lambda v, xsl=xsl: v.tensor_scalar(
                            xc[:], xsl, st1[:, 1:2], None, ALU.subtract),
                           reads=[xt, st1], writes=[xc])
                        ve(lambda v: v.tensor_tensor(xsq[:], xc[:], xc[:],
                                                     ALU.mult),
                           reads=[xc], writes=[xsq])
                        ve(lambda v: v.reduce_sum(st1[:, 2:3], xsq[:],
                                                  axis=AX.X),
                           reads=[xsq], writes=[st1])
                        se(lambda s: s.activation(st1[:, 3:4], st1[:, 2:3],
                                                  ACT.Sqrt, bias=f32(1e-5),
                                                  scale=f32(1.0 / D)),
                           reads=[st1], writes=[st1])
                        ve(lambda v: v.reciprocal(st1[:, 4:5], st1[:, 3:4]),
                           reads=[st1], writes=[st1])
                        ve(lambda v: v.tensor_scalar(xc[:], xc[:],
                                                     st1[:, 4:5], None,
                                                     ALU.mult),
                           reads=[xc, st1], writes=[xc])
                        ve(lambda v: v.tensor_copy(xnh[:], xc[:]),
                           reads=[xc], writes=[xnh])
                        ve(lambda v: v.scalar_tensor_tensor(
                            xnl[:], xnh[:], -1.0, xc[:], ALU.mult, ALU.add),
                           reads=[xc, xnh], writes=[xnl])
                        for src16, fmdst in ((xnh, fmh), (xnl, fml)):
                            for ec in range(DC):
                                pe(lambda t, src16=src16, ec=ec: t.transpose(
                                    ptl[:, 0:128],
                                    src16[:, ec * 128:(ec + 1) * 128],
                                    id16[:]),
                                   reads=[src16, id16], writes=[ptl])
                                ve(lambda v, fmdst=fmdst, ec=ec, i=i:
                                   v.tensor_copy(
                                       fmdst[:, ec * 512 + i * 128:
                                             ec * 512 + (i + 1) * 128],
                                       ptl[:, 0:128]),
                                   reads=[ptl], writes=[fmdst])
                    for ec in range(DC):
                        dstore(xnh_d[ec * 128:(ec + 1) * 128, t0:t0 + 512],
                               fmh[:, ec * 512:(ec + 1) * 512], reads=[fmh])
                        dstore(xnl_d[ec * 128:(ec + 1) * 128, t0:t0 + 512],
                               fml[:, ec * 512:(ec + 1) * 512], reads=[fml])
                plan.barrier()
                plan.flush()

            with contextlib.ExitStack() as _es:
                x16h = _es.enter_context(nc.sbuf_tensor("x16h", [128, DC * 512], F16))
                x16l = _es.enter_context(nc.sbuf_tensor("x16l", [128, DC * 512], F16))
                rr = _es.enter_context(nc.sbuf_tensor("rr", [128, 512], F32))
                zc = _es.enter_context(nc.sbuf_tensor("zc", [128, 512], F32))
                zs = _es.enter_context(nc.sbuf_tensor("zs", [128, 512], F32))
                tg32 = _es.enter_context(nc.sbuf_tensor("tg32", [128, 512], F32))
                tgh = _es.enter_context(nc.sbuf_tensor("tgh", [128, 512], F16))
                tgl = _es.enter_context(nc.sbuf_tensor("tgl", [128, 512], F16))
                vh = _es.enter_context(nc.sbuf_tensor("vh", [128, 512], F16))
                vl = _es.enter_context(nc.sbuf_tensor("vl", [128, 512], F16))
                vash = _es.enter_context(nc.sbuf_tensor("vash", [128, 4 * 1024], F16))
                vasl = _es.enter_context(nc.sbuf_tensor("vasl", [128, 4 * 1024], F16))
                pj0 = _es.enter_context(nc.psum_tensor("pj0", [128, 512], F32))
                pj1 = _es.enter_context(nc.psum_tensor("pj1", [128, 512], F32))
                pj2 = _es.enter_context(nc.psum_tensor("pj2", [128, 512], F32))
                pj3 = _es.enter_context(nc.psum_tensor("pj3", [128, 512], F32))
                ptv = _es.enter_context(nc.psum_tensor("ptv", [128, 1024], F16))
                pjs = [pj0, pj1, pj2, pj3]

                def trig_chain(rr_t, zbuf, shift, m):
                    se(lambda s: s.activation(zbuf[:], rr_t[:], ACT.Abs,
                                              bias=f32(-shift) if shift else 0.0),
                       reads=[rr_t], writes=[zbuf])
                    for j in range(m - 1, -1, -1):
                        se(lambda s, j=j: s.activation(
                            zbuf[:], zbuf[:], ACT.Abs, bias=f32(-(2.0 ** j))),
                           reads=[zbuf], writes=[zbuf])
                    se(lambda s: s.activation(zbuf[:], zbuf[:], ACT.Abs,
                                              bias=f32(-0.5)),
                       reads=[zbuf], writes=[zbuf])
                    se(lambda s: s.activation(tg32[:], zbuf[:], ACT.Sin,
                                              bias=f32(-PI / 2),
                                              scale=f32(TWO_PI)),
                       reads=[zbuf], writes=[tg32])

                def store_feat(nm, ec, t0):
                    ve(lambda v: v.tensor_copy(tgh[:], tg32[:]),
                       reads=[tg32], writes=[tgh])
                    ve(lambda v: v.scalar_tensor_tensor(
                        tgl[:], tgh[:], -1.0, tg32[:], ALU.mult, ALU.add),
                       reads=[tg32, tgh], writes=[tgl])
                    dstore(feat_d[nm][0][ec * 128:(ec + 1) * 128, t0:t0 + 512],
                           tgh[:], reads=[tgh])
                    dstore(feat_d[nm][1][ec * 128:(ec + 1) * 128, t0:t0 + 512],
                           tgl[:], reads=[tgl])

                for blk in range(T // 512):
                    t0 = blk * 512
                    for ec in range(DC):
                        dload(x16h[:, ec * 512:(ec + 1) * 512],
                              xnh_d[ec * 128:(ec + 1) * 128, t0:t0 + 512],
                              writes=[x16h])
                        dload(x16l[:, ec * 512:(ec + 1) * 512],
                              xnl_d[ec * 128:(ec + 1) * 128, t0:t0 + 512],
                              writes=[x16l])
                    for p in "qkv":
                        wh_t, wl_t = wT[p]
                        bh_t, bl_t = brow[p]
                        for ec in range(DC):
                            ps = pjs[ec % 4]

                            def mm_proj(t, ps=ps, wh_t=wh_t, wl_t=wl_t,
                                        bh_t=bh_t, bl_t=bl_t, ec=ec):
                                t.matmul(ps[:],
                                         bh_t[:1, ec * 128:(ec + 1) * 128],
                                         ones16[:1, 0:512], start=True,
                                         stop=False)
                                t.matmul(ps[:],
                                         bl_t[:1, ec * 128:(ec + 1) * 128],
                                         ones16[:1, 0:512], start=False,
                                         stop=False)
                                last = None
                                for dc in range(DC):
                                    wsl = slice(dc * 1024 + ec * 128,
                                                dc * 1024 + (ec + 1) * 128)
                                    xsl = slice(dc * 512, (dc + 1) * 512)
                                    t.matmul(ps[:], wh_t[:, wsl],
                                             x16h[:, xsl], start=False,
                                             stop=False)
                                    t.matmul(ps[:], wh_t[:, wsl],
                                             x16l[:, xsl], start=False,
                                             stop=False)
                                    last = t.matmul(ps[:], wl_t[:, wsl],
                                                    x16h[:, xsl], start=False,
                                                    stop=dc == DC - 1)
                                return last

                            pe(mm_proj, reads=[wh_t, wl_t, bh_t, bl_t, x16h,
                                               x16l, ones16], writes=[ps])
                            if p in "qk":
                                ve(lambda v, ps=ps, ec=ec: v.tensor_scalar(
                                    rr[:], ps[:], chi[:, ec:ec + 1],
                                    pbz[:, ec:ec + 1], ALU.mult, ALU.add),
                                   reads=[ps, chi, pbz], writes=[rr])
                                ve(lambda v, ps=ps, ec=ec:
                                   v.scalar_tensor_tensor(
                                       rr[:], ps[:], clo[:, ec:ec + 1], rr[:],
                                       ALU.mult, ALU.add),
                                   reads=[ps, clo, rr], writes=[rr])
                                m = M_FOLDS[ec]
                                trig_chain(rr, zc, 0.0, m)
                                store_feat("cq" if p == "q" else "ck", ec, t0)
                                trig_chain(rr, zs, 0.25, m)
                                store_feat("sq" if p == "q" else "sk", ec, t0)
                            else:
                                ve(lambda v, ps=ps: v.tensor_copy(vh[:],
                                                                  ps[:]),
                                   reads=[ps], writes=[vh])
                                ve(lambda v, ps=ps: v.scalar_tensor_tensor(
                                    vl[:], vh[:], -1.0, ps[:], ALU.mult,
                                    ALU.add), reads=[ps, vh], writes=[vl])
                                for vsrc, vdst in ((vh, vash), (vl, vasl)):
                                    for tc4 in range(4):
                                        pe(lambda t, vsrc=vsrc, tc4=tc4:
                                           t.transpose(
                                               ptv[:, 0:128],
                                               vsrc[:, tc4 * 128:
                                                    (tc4 + 1) * 128],
                                               id16[:]),
                                           reads=[vsrc, id16], writes=[ptv])
                                        ve(lambda v, vdst=vdst, tc4=tc4,
                                           ec=ec: v.tensor_copy(
                                               vdst[:, tc4 * 1024 + ec * 128:
                                                    tc4 * 1024
                                                    + (ec + 1) * 128],
                                               ptv[:, 0:128]),
                                           reads=[ptv], writes=[vdst])
                    for tc4 in range(4):
                        dstore(vth_d[t0 + tc4 * 128:t0 + (tc4 + 1) * 128, :],
                               vash[:, tc4 * 1024:(tc4 + 1) * 1024],
                               reads=[vash])
                        dstore(vtl_d[t0 + tc4 * 128:t0 + (tc4 + 1) * 128, :],
                               vasl[:, tc4 * 1024:(tc4 + 1) * 1024],
                               reads=[vasl])
                plan.barrier()
                plan.flush()

        with contextlib.ExitStack() as _es:
            cqh_s = _es.enter_context(nc.sbuf_tensor("cqh_s", [128, DC * QQ], F16))
            cql_s = _es.enter_context(nc.sbuf_tensor("cql_s", [128, DC * QQ], F16))
            sqh_s = _es.enter_context(nc.sbuf_tensor("sqh_s", [128, DC * QQ], F16))
            sql_s = _es.enter_context(nc.sbuf_tensor("sql_s", [128, DC * QQ], F16))
            ckh_s = _es.enter_context(nc.sbuf_tensor("ckh_s", [128, DC * TB], F16))
            ckl_s = _es.enter_context(nc.sbuf_tensor("ckl_s", [128, DC * TB], F16))
            skh_s = _es.enter_context(nc.sbuf_tensor("skh_s", [128, DC * TB], F16))
            skl_s = _es.enter_context(nc.sbuf_tensor("skl_s", [128, DC * TB], F16))
            simr = _es.enter_context(nc.sbuf_tensor("simr", [128, 0 + 4 * 2048], F32))
            prob = _es.enter_context(nc.sbuf_tensor("prob", [128, 2048], F32))
            ph_s = _es.enter_context(nc.sbuf_tensor("ph_s", [128, 4 * 2048], F16))
            pl_s = _es.enter_context(nc.sbuf_tensor("pl_s", [128, 4 * 2048], F16))
            pth = _es.enter_context(nc.sbuf_tensor("pth", [128, 128], F16))
            ptt = _es.enter_context(nc.sbuf_tensor("ptt", [128, 128], F16))
            vbh = _es.enter_context(nc.sbuf_tensor("vbh", [128, TC * 1024], F16))
            vbl = _es.enter_context(nc.sbuf_tensor("vbl", [128, TC * 1024], F16))
            smax = _es.enter_context(nc.sbuf_tensor("smax", [128, 8], F32))
            aout = _es.enter_context(nc.sbuf_tensor("aout", [128, 1024], F32))
            pa0 = _es.enter_context(nc.psum_tensor("pa0", [128, 512], F32))
            pa1 = _es.enter_context(nc.psum_tensor("pa1", [128, 512], F32))
            pa2 = _es.enter_context(nc.psum_tensor("pa2", [128, 512], F32))
            pa3 = _es.enter_context(nc.psum_tensor("pa3", [128, 512], F32))
            ptp = _es.enter_context(nc.psum_tensor("ptp", [128, 1024], F16))
            assert SB <= 4 and S <= 2048
            pas = [pa0, pa1, pa2, pa3]
            qarr = {"cq": (cqh_s, cql_s), "sq": (sqh_s, sql_s)}
            karr = {"ck": (ckh_s, ckl_s), "sk": (skh_s, skl_s)}
            for seq in range(SEQS):
                sbase = seq * S
                for qq in range(NQQ):
                    q0 = sbase + qq * QQ
                    for nm in ("cq", "sq"):
                        bh2, bl2 = qarr[nm]
                        sh_d, sl_d = feat_d[nm]
                        for ec in range(DC):
                            dload(bh2[:, ec * QQ:(ec + 1) * QQ],
                                  sh_d[ec * 128:(ec + 1) * 128, q0:q0 + QQ],
                                  writes=[bh2])
                            dload(bl2[:, ec * QQ:(ec + 1) * QQ],
                                  sl_d[ec * 128:(ec + 1) * 128, q0:q0 + QQ],
                                  writes=[bl2])
                    for tb in range(NTB):
                        k0 = sbase + tb * TB
                        for nm in ("ck", "sk"):
                            bh2, bl2 = karr[nm]
                            sh_d, sl_d = feat_d[nm]
                            for ec in range(DC):
                                dload(bh2[:, ec * TB:(ec + 1) * TB],
                                      sh_d[ec * 128:(ec + 1) * 128,
                                           k0:k0 + TB], writes=[bh2])
                                dload(bl2[:, ec * TB:(ec + 1) * TB],
                                      sl_d[ec * 128:(ec + 1) * 128,
                                           k0:k0 + TB], writes=[bl2])
                        for sbi in range(SB):
                            ps = pas[sbi % 4]

                            def mm_sim(t, ps=ps, sbi=sbi):
                                first = True
                                last = None
                                for qnm, knm in (("cq", "ck"), ("sq", "sk")):
                                    qh2, ql2 = qarr[qnm]
                                    kh2, kl2 = karr[knm]
                                    for ec in range(DC):
                                        qsl = slice(ec * QQ + sbi * 128,
                                                    ec * QQ + (sbi + 1) * 128)
                                        ksl = slice(ec * TB, (ec + 1) * TB)
                                        t.matmul(ps[:, 0:TB], qh2[:, qsl],
                                                 kh2[:, ksl], start=first,
                                                 stop=False)
                                        first = False
                                        t.matmul(ps[:, 0:TB], qh2[:, qsl],
                                                 kl2[:, ksl], start=False,
                                                 stop=False)
                                        last = t.matmul(
                                            ps[:, 0:TB], ql2[:, qsl],
                                            kh2[:, ksl], start=False,
                                            stop=(qnm == "sq"
                                                  and ec == DC - 1))
                                return last

                            pe(mm_sim, reads=[cqh_s, cql_s, sqh_s, sql_s,
                                              ckh_s, ckl_s, skh_s, skl_s],
                               writes=[ps])
                            ve(lambda v, ps=ps, sbi=sbi, tb=tb: v.tensor_copy(
                                simr[:, sbi * S + tb * TB:
                                     sbi * S + (tb + 1) * TB], ps[:, 0:TB]),
                               reads=[ps], writes=[simr])
                    for sbi in range(SB):
                        ssl = slice(sbi * S, (sbi + 1) * S)
                        ve(lambda v, ssl=ssl, sbi=sbi: v.reduce_max(
                            smax[:, sbi:sbi + 1], simr[:, ssl], axis=AX.X,
                            negate=True), reads=[simr], writes=[smax])
                        ve(lambda v, sbi=sbi: v.tensor_scalar(
                            smax[:, 4 + sbi:5 + sbi], smax[:, sbi:sbi + 1],
                            float(f32(1.0 / 32.0)), None, ALU.mult),
                           reads=[smax], writes=[smax])
                        se(lambda s, ssl=ssl, sbi=sbi: s.activation(
                            prob[:, 0:S], simr[:, ssl], ACT.Exp,
                            bias=smax[:, 4 + sbi:5 + sbi],
                            scale=f32(1.0 / 32.0)),
                           reads=[simr, smax], writes=[prob])
                        ve(lambda v, sbi=sbi: v.reduce_sum(
                            smax[:, sbi:sbi + 1], prob[:, 0:S], axis=AX.X),
                           reads=[prob], writes=[smax])
                        ve(lambda v, sbi=sbi: v.reciprocal(
                            smax[:, 4 + sbi:5 + sbi], smax[:, sbi:sbi + 1]),
                           reads=[smax], writes=[smax])
                        ve(lambda v, ssl=ssl: v.tensor_copy(
                            ph_s[:, ssl], prob[:, 0:S]),
                           reads=[prob], writes=[ph_s])
                        ve(lambda v, ssl=ssl: v.scalar_tensor_tensor(
                            pl_s[:, ssl], ph_s[:, ssl], -1.0, prob[:, 0:S],
                            ALU.mult, ALU.add),
                           reads=[prob, ph_s], writes=[pl_s])
                    for pair in range(SB // 2):
                        sb0 = pair * 2
                        for tb in range(NTB):
                            k0r = sbase + tb * TB
                            for tci in range(TC):
                                dload(vbh[:, tci * 1024:(tci + 1) * 1024],
                                      vth_d[k0r + tci * 128:
                                            k0r + (tci + 1) * 128, :],
                                      writes=[vbh])
                                dload(vbl[:, tci * 1024:(tci + 1) * 1024],
                                      vtl_d[k0r + tci * 128:
                                            k0r + (tci + 1) * 128, :],
                                      writes=[vbl])
                            for sbi in range(sb0, sb0 + 2):
                                for tci in range(TC):
                                    tsl = slice(sbi * S + tb * TB + tci * 128,
                                                sbi * S + tb * TB
                                                + (tci + 1) * 128)
                                    pe(lambda t, tsl=tsl: t.transpose(
                                        ptp[:, 0:128], ph_s[:, tsl], id16[:]),
                                       reads=[ph_s, id16], writes=[ptp])
                                    ve(lambda v: v.tensor_copy(
                                        pth[:], ptp[:, 0:128]),
                                       reads=[ptp], writes=[pth])
                                    pe(lambda t, tsl=tsl: t.transpose(
                                        ptp[:, 0:128], pl_s[:, tsl], id16[:]),
                                       reads=[pl_s, id16], writes=[ptp])
                                    ve(lambda v: v.tensor_copy(
                                        ptt[:], ptp[:, 0:128]),
                                       reads=[ptp], writes=[ptt])
                                    for dh in range(2):
                                        pv = pas[(sbi - sb0) * 2 + dh]
                                        vsl = slice(tci * 1024 + dh * 512,
                                                    tci * 1024
                                                    + (dh + 1) * 512)
                                        strt = tb == 0 and tci == 0
                                        stp = (tb == NTB - 1
                                               and tci == TC - 1)

                                        def mm_pv(t, pv=pv, vsl=vsl,
                                                  strt=strt, stp=stp):
                                            t.matmul(pv[:], pth[:],
                                                     vbh[:, vsl], start=strt,
                                                     stop=False)
                                            t.matmul(pv[:], pth[:],
                                                     vbl[:, vsl], start=False,
                                                     stop=False)
                                            return t.matmul(
                                                pv[:], ptt[:], vbh[:, vsl],
                                                start=False, stop=stp)

                                        pe(mm_pv,
                                           reads=[pth, ptt, vbh, vbl],
                                           writes=[pv])
                        for sbi in range(sb0, sb0 + 2):
                            for dh in range(2):
                                pv = pas[(sbi - sb0) * 2 + dh]
                                ve(lambda v, pv=pv, dh=dh, sbi=sbi:
                                   v.tensor_scalar(
                                       aout[:, dh * 512:(dh + 1) * 512],
                                       pv[:], smax[:, 4 + sbi:5 + sbi], None,
                                       ALU.mult),
                                   reads=[pv, smax], writes=[aout])
                            dstore(attn_d[q0 + sbi * 128:
                                          q0 + (sbi + 1) * 128, :],
                                   aout[:], reads=[aout])
            plan.barrier()
            plan.flush()

        with contextlib.ExitStack() as _es:
            k16h_s = _es.enter_context(nc.sbuf_tensor("k16h_s", [128, DC * 1024], F16))
            k16l_s = _es.enter_context(nc.sbuf_tensor("k16l_s", [128, DC * 1024], F16))
            x16h4 = _es.enter_context(nc.sbuf_tensor("x16h4", [128, DC * 512], F16))
            x16l4 = _es.enter_context(nc.sbuf_tensor("x16l4", [128, DC * 512], F16))
            xres = _es.enter_context(nc.sbuf_tensor("xres", [128, 4 * 1024], F32))
            ares = _es.enter_context(nc.sbuf_tensor("ares", [128, 4 * 1024], F32))
            t1h = _es.enter_context(nc.sbuf_tensor("t1h", [16, 512], F16))
            t1l = _es.enter_context(nc.sbuf_tensor("t1l", [16, 512], F16))
            oacc = _es.enter_context(nc.sbuf_tensor("oacc", [128, 1024], F32))
            pf0 = _es.enter_context(nc.psum_tensor("pf0", [128, 512], F32))
            pf1 = _es.enter_context(nc.psum_tensor("pf1", [128, 512], F32))
            pt1 = _es.enter_context(nc.psum_tensor("pt1", [128, 512], F32))
            for dc in range(DC):
                dload(k16h_s[:, dc * 1024:(dc + 1) * 1024],
                      kqh_d[dc * 128:(dc + 1) * 128, :], writes=[k16h_s])
                dload(k16l_s[:, dc * 1024:(dc + 1) * 1024],
                      kql_d[dc * 128:(dc + 1) * 128, :], writes=[k16l_s])
            for blk in range(T // 512):
                t0 = blk * 512
                for ec in range(DC):
                    dload(x16h4[:, ec * 512:(ec + 1) * 512],
                          xnh_d[ec * 128:(ec + 1) * 128, t0:t0 + 512],
                          writes=[x16h4])
                    dload(x16l4[:, ec * 512:(ec + 1) * 512],
                          xnl_d[ec * 128:(ec + 1) * 128, t0:t0 + 512],
                          writes=[x16l4])
                for i in range(4):
                    dload(xres[:, i * 1024:(i + 1) * 1024],
                          x_d[t0 + i * 128:t0 + (i + 1) * 128, :],
                          writes=[xres])
                    dload(ares[:, i * 1024:(i + 1) * 1024],
                          attn_d[t0 + i * 128:t0 + (i + 1) * 128, :],
                          writes=[ares])

                def mm_t1(t):
                    last = None
                    for dc in range(DC):
                        asl = slice(dc * 16, (dc + 1) * 16)
                        xsl = slice(dc * 512, (dc + 1) * 512)
                        t.matmul(pt1[:16, :], a16h[:, asl], x16h4[:, xsl],
                                 start=dc == 0, stop=False)
                        t.matmul(pt1[:16, :], a16h[:, asl], x16l4[:, xsl],
                                 start=False, stop=False)
                        last = t.matmul(pt1[:16, :], a16l[:, asl],
                                        x16h4[:, xsl], start=False,
                                        stop=dc == DC - 1)
                    return last

                pe(mm_t1, reads=[a16h, a16l, x16h4, x16l4], writes=[pt1])
                ve(lambda v: v.tensor_copy(t1h[:16, :], pt1[:16, :]),
                   reads=[pt1], writes=[t1h])
                ve(lambda v: v.scalar_tensor_tensor(t1l[:16, :], t1h[:16, :],
                                                    -1.0, pt1[:16, :],
                                                    ALU.mult, ALU.add),
                   reads=[pt1, t1h], writes=[t1l])
                for i in range(4):
                    for dh in range(2):
                        pf = pf0 if dh == 0 else pf1
                        fsl = slice(dh * 512, (dh + 1) * 512)

                        def mm_ffn(t, pf=pf, fsl=fsl, i=i):
                            fh, fl = brow["f"]
                            t.matmul(pf[:], ones16[:1, 0:128], fh[:1, fsl],
                                     start=True, stop=False)
                            t.matmul(pf[:], ones16[:1, 0:128], fl[:1, fsl],
                                     start=False, stop=False)
                            for dc in range(DC):
                                ksl = slice(dc * 1024 + fsl.start,
                                            dc * 1024 + fsl.stop)
                                xsl = slice(dc * 512 + i * 128,
                                            dc * 512 + (i + 1) * 128)
                                t.matmul(pf[:], x16h4[:, xsl], k16h_s[:, ksl],
                                         start=False, stop=False)
                                t.matmul(pf[:], x16h4[:, xsl], k16l_s[:, ksl],
                                         start=False, stop=False)
                                t.matmul(pf[:], x16l4[:, xsl], k16h_s[:, ksl],
                                         start=False, stop=False)
                            tsl = slice(i * 128, (i + 1) * 128)
                            t.matmul(pf[:], t1h[:16, tsl], b16h[:16, fsl],
                                     start=False, stop=False)
                            t.matmul(pf[:], t1h[:16, tsl], b16l[:16, fsl],
                                     start=False, stop=False)
                            return t.matmul(pf[:], t1l[:16, tsl],
                                            b16h[:16, fsl], start=False,
                                            stop=True)

                        pe(mm_ffn, reads=[x16h4, x16l4, k16h_s, k16l_s, t1h,
                                          t1l, b16h, b16l, ones16,
                                          brow["f"][0], brow["f"][1]],
                           writes=[pf])
                    ve(lambda v, i=i: v.tensor_tensor(
                        oacc[:], xres[:, i * 1024:(i + 1) * 1024],
                        ares[:, i * 1024:(i + 1) * 1024], ALU.add),
                       reads=[xres, ares], writes=[oacc])
                    ve(lambda v: v.tensor_tensor(oacc[:, 0:512],
                                                 oacc[:, 0:512], pf0[:],
                                                 ALU.add),
                       reads=[pf0, oacc], writes=[oacc])
                    ve(lambda v: v.tensor_tensor(oacc[:, 512:1024],
                                                 oacc[:, 512:1024], pf1[:],
                                                 ALU.add),
                       reads=[pf1, oacc], writes=[oacc])
                    dstore(out_d[t0 + i * 128:t0 + (i + 1) * 128, :],
                           oacc[:], reads=[oacc])
            plan.barrier()
            plan.flush()
    return nc


_NC_CACHE = {}


def _get_nc(key):
    if key not in _NC_CACHE:
        _NC_CACHE[key] = build_nc(*key)
    return _NC_CACHE[key]


def make_in_maps(L0_attn_x, L0_attn_Wq, L0_attn_bq, L0_attn_Wk, L0_attn_bk,
                 L0_attn_Wv, L0_attn_bv, L0_attn_phase_bias, L0_ffn_kernel,
                 L0_ffn_bias, L0_ffn_a, L0_ffn_b, n_cores=N_CORES):
    x = np.ascontiguousarray(np.asarray(L0_attn_x, dtype=np.float32))
    B, S, _ = x.shape
    SEQS = B // n_cores
    common = {
        "wq": np.ascontiguousarray(np.asarray(L0_attn_Wq, np.float32)),
        "wk": np.ascontiguousarray(np.asarray(L0_attn_Wk, np.float32)),
        "wv": np.ascontiguousarray(np.asarray(L0_attn_Wv, np.float32)),
        "bq": np.asarray(L0_attn_bq, np.float32).reshape(1, D),
        "bk": np.asarray(L0_attn_bk, np.float32).reshape(1, D),
        "bv": np.asarray(L0_attn_bv, np.float32).reshape(1, D),
        "pb_pp": _pp(np.asarray(L0_attn_phase_bias, np.float32)),
        "kern": np.ascontiguousarray(np.asarray(L0_ffn_kernel, np.float32)),
        "fb": np.asarray(L0_ffn_bias, np.float32).reshape(1, D),
        "a": np.ascontiguousarray(np.asarray(L0_ffn_a, np.float32)),
        "bl": np.ascontiguousarray(np.asarray(L0_ffn_b, np.float32)),
        "chi": _pp(C2PI_HI),
        "clo": _pp(C2PI_LO),
        "id16": np.eye(128, dtype=np.float16),
        "idf": np.eye(128, dtype=np.float32),
    }
    in_maps = []
    for c in range(n_cores):
        m = dict(common)
        m["x"] = np.ascontiguousarray(
            x[c * SEQS:(c + 1) * SEQS].reshape(SEQS * S, D))
        in_maps.append(m)
    return in_maps, SEQS, S


def kernel(**inputs):
    in_maps, SEQS, S = make_in_maps(**inputs)
    nc = _get_nc((SEQS, S, 512, 256))
    res = run_bass_kernel_spmd(nc, in_maps, core_ids=list(range(N_CORES)))
    outs = [np.asarray(r["out"]).reshape(SEQS, S, D) for r in res.results]
    return np.concatenate(outs, axis=0)


# version 03
# speedup vs baseline: 1.6213x; 1.6213x over previous
import math
import contextlib
import numpy as np
import concourse.bass as bass
import concourse.mybir as mybir
from concourse.bass_utils import run_bass_kernel_spmd

F32 = mybir.dt.float32
F16 = mybir.dt.float16
ALU = mybir.AluOpType
ACT = mybir.ActivationFunctionType
AX = mybir.AxisListType

f32 = np.float32
PI = math.pi
TWO_PI = 2.0 * math.pi

D = 1024
DC = D // 128
N_CORES = 8


def _host_consts():
    wl = np.arange(1, D + 1, dtype=np.float32) * f32(TWO_PI / D)
    inv_wl = (f32(1.0) / (wl + f32(1e-8))).astype(np.float32)
    c = inv_wl.astype(np.float64) / TWO_PI
    c_hi = c.astype(np.float32)
    c_lo = (c - c_hi.astype(np.float64)).astype(np.float32)
    QB, PBM = 48.0, 3.2
    m_folds = []
    for ec in range(DC):
        rr = (QB * float(inv_wl[ec * 128:(ec + 1) * 128].max()) + PBM) / TWO_PI
        m_folds.append(max(0, int(math.ceil(math.log2(max(rr, 1.0))))))
    return inv_wl, c_hi, c_lo, m_folds


INV_WL, C2PI_HI, C2PI_LO, M_FOLDS = _host_consts()


def _pp(v):
    return np.ascontiguousarray(v.reshape(DC, 128).T)


ENGS = ("sync", "vector", "scalar", "tensor", "gpsimd")


class Ctr:
    def __init__(self, es, nc, name):
        self.sem = es.enter_context(nc.semaphore(name))
        self.v = 0


class Plan:

    def __init__(self, es, nc, block):
        self.nc = nc
        self.es = es
        self.block = block
        self.steps = {e: [] for e in ENGS}
        self.c = {e: Ctr(es, nc, "c_" + e) for e in ENGS}
        self.all_ctrs = list(self.c.values())
        self.buf = {}
        self.dma_ctrs = {}
        self.last_wait = {e: {} for e in ENGS}

    def dma_ctr(self, buf):
        k = id(buf)
        if k not in self.dma_ctrs:
            ctr = Ctr(self.es, self.nc, "d%d" % len(self.dma_ctrs))
            self.dma_ctrs[k] = ctr
            self.all_ctrs.append(ctr)
        return self.dma_ctrs[k]

    def _dep(self, reads, writes):
        waits = {}
        for b in reads:
            st = self.buf.get(id(b))
            if st and st["ready"]:
                c, v = st["ready"]
                waits[c] = max(waits.get(c, 0), v)
        for b in writes:
            st = self.buf.get(id(b))
            if st:
                if st["ready"]:
                    c, v = st["ready"]
                    waits[c] = max(waits.get(c, 0), v)
                for c, v in st["cons"].items():
                    waits[c] = max(waits.get(c, 0), v)
        return waits

    def add(self, eng, emit, reads=(), writes=(), extra_waits=(), dma=0,
            nop=False, dma_buf=None):
        ctr = self.dma_ctr(dma_buf) if dma_buf is not None else self.c[eng]
        waits = self._dep(reads, writes)
        for (c2, v2) in extra_waits:
            waits[c2] = max(waits.get(c2, 0), v2)
        n = 16 * dma if dma else 1
        token = (ctr, ctr.v + n)
        self.steps[eng].append(
            {"emit": emit, "waits": waits, "ctr": ctr, "dma": dma, "nop": nop,
             "waitonly": False})
        ctr.v += n
        for b in reads:
            st = self.buf.setdefault(id(b), {"ready": None, "cons": {}})
            st["cons"][token[0]] = max(st["cons"].get(token[0], 0), token[1])
        for b in writes:
            self.buf[id(b)] = {"ready": token, "cons": {}}
        return token

    def barrier(self):
        vals = [(c, c.v) for c in self.all_ctrs]
        for eng in ENGS:
            ew = [(c, v) for c, v in vals if c is not self.c[eng]]
            waits = {}
            for (c2, v2) in ew:
                waits[c2] = max(waits.get(c2, 0), v2)
            self.steps[eng].append({"emit": None, "waits": waits,
                                    "ctr": self.c[eng], "dma": 0,
                                    "nop": False, "waitonly": True})

    def flush(self):
        for eng in ENGS:
            steps = self.steps[eng]
            if not steps:
                continue
            self.steps[eng] = []
            my = self.c[eng]
            lw = self.last_wait[eng]

            def section(engine, steps=steps, my=my, lw=lw):
                for st in steps:
                    for c, v in sorted(st["waits"].items(),
                                       key=lambda kv: id(kv[0])):
                        if lw.get(id(c), -1) < v:
                            engine.wait_ge(c.sem, v)
                            lw[id(c)] = v
                    if st["waitonly"]:
                        pass
                    elif st["nop"]:
                        engine.nop().then_inc(st["ctr"].sem, 1)
                    elif st["dma"]:
                        st["emit"](engine, st["ctr"].sem)
                    else:
                        st["emit"](engine).then_inc(st["ctr"].sem, 1)

            getattr(self.block, eng)(section)


def build_nc(SEQS=4, S=2048, QQ=512, TB=512, dbg=False):
    T = SEQS * S
    SB = QQ // 128
    NQQ = S // QQ
    NTB = S // TB
    TC = TB // 128
    NSC = S // 128
    assert SB % 2 == 0 and S % QQ == 0 and S % TB == 0
    nc = bass.Bass("TRN2", target_bir_lowering=False)

    x_d = nc.dram_tensor("x", [T, D], F32, kind="ExternalInput").ap()
    w_d = {p: nc.dram_tensor("w" + p, [D, D], F32, kind="ExternalInput").ap()
           for p in "qkv"}
    b_d = {p: nc.dram_tensor("b" + p, [1, D], F32, kind="ExternalInput").ap()
           for p in "qkv"}
    pb_pp_d = nc.dram_tensor("pb_pp", [128, DC], F32, kind="ExternalInput").ap()
    kern_d = nc.dram_tensor("kern", [D, D], F32, kind="ExternalInput").ap()
    fb_d = nc.dram_tensor("fb", [1, D], F32, kind="ExternalInput").ap()
    a_d = nc.dram_tensor("a", [D, 16], F32, kind="ExternalInput").ap()
    bl_d = nc.dram_tensor("bl", [16, D], F32, kind="ExternalInput").ap()
    chi_d = nc.dram_tensor("chi", [128, DC], F32, kind="ExternalInput").ap()
    clo_d = nc.dram_tensor("clo", [128, DC], F32, kind="ExternalInput").ap()
    id16_d = nc.dram_tensor("id16", [128, 128], F16, kind="ExternalInput").ap()
    idf_d = nc.dram_tensor("idf", [128, 128], F32, kind="ExternalInput").ap()
    out_d = nc.dram_tensor("out", [T, D], F32, kind="ExternalOutput").ap()

    def scr(name, shape, dt):
        return nc.dram_tensor(name, shape, dt,
                              kind="ExternalOutput" if dbg else "Internal").ap()

    xnh_d, xnl_d = scr("xnh", [D, T], F16), scr("xnl", [D, T], F16)
    feat_d = {nm: (scr(nm + "h", [D, T], F16), scr(nm + "l", [D, T], F16))
              for nm in ("cq", "sq", "ck", "sk")}
    vth_d, vtl_d = scr("vth", [T, D], F16), scr("vtl", [T, D], F16)
    attn_d = scr("attn", [T, D], F32)
    kqh_d, kql_d = scr("kqh", [D, D], F16), scr("kql", [D, D], F16)

    es = contextlib.ExitStack()
    with es:
        def sb(name, shape, dt):
            return es.enter_context(nc.sbuf_tensor(name, shape, dt))

        id16 = sb("s_id16", [128, 128], F16)
        idf = sb("s_idf", [128, 128], F32)
        chi = sb("s_chi", [128, DC], F32)
        clo = sb("s_clo", [128, DC], F32)
        pbz = sb("pbz", [128, DC], F32)
        ones16 = sb("ones16", [1, 1024], F16)
        onesf = sb("onesf", [1, 128], F32)
        brow = {p: (sb("b" + p + "hr", [1, D], F16),
                    sb("b" + p + "lr", [1, D], F16))
                for p in ("q", "k", "v", "f")}
        a16h = sb("a16h", [128, DC * 16], F16)
        a16l = sb("a16l", [128, DC * 16], F16)
        b16h = sb("b16h", [16, D], F16)
        b16l = sb("b16l", [16, D], F16)
        rowbuf = sb("rowbuf", [1, D], F32)

        block = es.enter_context(nc.Block())
        plan = Plan(es, nc, block)

        def dload(dst_ap, src_ap, reads=(), writes=()):
            assert len(writes) == 1
            def emit(engine, sem, dst=dst_ap, src=src_ap):
                engine.dma_start(out=dst, in_=src).then_inc(sem, 16)
            return plan.add("sync", emit, reads=reads, writes=writes, dma=1,
                            dma_buf=writes[0])

        def dstore(dst_ap, src_ap, reads=(), writes=()):
            assert len(reads) == 1
            def emit(engine, sem, dst=dst_ap, src=src_ap):
                engine.dma_start(out=dst, in_=src).then_inc(sem, 16)
            return plan.add("gpsimd", emit, reads=reads, writes=writes, dma=1,
                            dma_buf=reads[0])

        ve = lambda emit, **kw: plan.add("vector", emit, **kw)
        se = lambda emit, **kw: plan.add("scalar", emit, **kw)
        pe = lambda emit, **kw: plan.add("tensor", emit, **kw)
        gp = lambda emit, **kw: plan.add("gpsimd", emit, **kw)

        dload(id16[:], id16_d, writes=[id16])
        dload(idf[:], idf_d, writes=[idf])
        dload(chi[:], chi_d, writes=[chi])
        dload(clo[:], clo_d, writes=[clo])
        dload(pbz[:], pb_pp_d, writes=[pbz])
        ve(lambda v: v.memset(ones16[:1, :], 1.0), writes=[ones16])
        ve(lambda v: v.memset(onesf[:1, :], 1.0), writes=[onesf])
        ve(lambda v: v.tensor_scalar(pbz[:], pbz[:], float(f32(1.0 / TWO_PI)),
                                     None, ALU.mult), reads=[pbz], writes=[pbz])

        for p, src in (("q", b_d["q"]), ("k", b_d["k"]), ("v", b_d["v"]),
                       ("f", fb_d)):
            dload(rowbuf[:1, :], src, writes=[rowbuf])
            h, l = brow[p]
            ve(lambda v, h=h: v.tensor_copy(h[:1, :], rowbuf[:1, :]),
               reads=[rowbuf], writes=[h])
            ve(lambda v, h=h, l=l: v.scalar_tensor_tensor(
                l[:1, :], h[:1, :], -1.0, rowbuf[:1, :], ALU.mult, ALU.add),
               reads=[rowbuf, h], writes=[l])

        with contextlib.ExitStack() as _es:
            abuf = _es.enter_context(nc.sbuf_tensor("abuf", [128, DC * 16], F32))
            bbuf = _es.enter_context(nc.sbuf_tensor("bbuf", [16, D], F32))
            for dc in range(DC):
                dload(abuf[:, dc * 16:(dc + 1) * 16],
                      a_d[dc * 128:(dc + 1) * 128, :], writes=[abuf])
            ve(lambda v: v.tensor_copy(a16h[:], abuf[:]), reads=[abuf],
               writes=[a16h])
            ve(lambda v: v.scalar_tensor_tensor(a16l[:], a16h[:], -1.0,
                                                abuf[:], ALU.mult, ALU.add),
               reads=[abuf, a16h], writes=[a16l])
            dload(bbuf[:16, :], bl_d, writes=[bbuf])
            ve(lambda v: v.tensor_copy(b16h[:16, :], bbuf[:16, :]),
               reads=[bbuf], writes=[b16h])
            ve(lambda v: v.scalar_tensor_tensor(b16l[:16, :], b16h[:16, :],
                                                -1.0, bbuf[:16, :], ALU.mult,
                                                ALU.add),
               reads=[bbuf, b16h], writes=[b16l])
            plan.barrier()
            plan.flush()

        with contextlib.ExitStack() as _es:
            kw = _es.enter_context(nc.sbuf_tensor("kw", [128, DC * 1024], F32))
            qs = _es.enter_context(nc.sbuf_tensor("qs", [128, 8], F32))
            qz = _es.enter_context(nc.sbuf_tensor("qz", [128, 1024], F32))
            qg = _es.enter_context(nc.sbuf_tensor("qg", [128, 1024], F32))
            qt = _es.enter_context(nc.sbuf_tensor("qt", [128, 1024], F32))
            k16a = _es.enter_context(nc.sbuf_tensor("k16a", [128, 1024], F16))
            k16b = _es.enter_context(nc.sbuf_tensor("k16b", [128, 1024], F16))
            qrow = _es.enter_context(nc.sbuf_tensor("qrow", [1, 128], F32))
            pq0 = _es.enter_context(nc.psum_tensor("pq0", [128, 512], F32))
            for dc in range(DC):
                dload(kw[:, dc * 1024:(dc + 1) * 1024],
                      kern_d[dc * 128:(dc + 1) * 128, :], writes=[kw])
            ve(lambda v: v.reduce_max(qs[:, 0:1], kw[:], axis=AX.X,
                                      apply_absolute_value=True),
               reads=[kw], writes=[qs])
            pe(lambda t: t.transpose(pq0[:1, 0:128], qs[:, 0:1], idf[:]),
               reads=[qs, idf], writes=[pq0])
            ve(lambda v: v.tensor_copy(qrow[:1, :], pq0[:1, 0:128]),
               reads=[pq0], writes=[qrow])
            ve(lambda v: v.reduce_max(qrow[:1, 0:1], qrow[:1, :], axis=AX.X),
               reads=[qrow], writes=[qrow])
            pe(lambda t: t.matmul(pq0[:, 0:1], onesf[:1, :], qrow[:1, 0:1],
                                  start=True, stop=True),
               reads=[qrow, onesf], writes=[pq0])
            ve(lambda v: v.tensor_copy(qs[:, 0:1], pq0[:, 0:1]),
               reads=[pq0], writes=[qs])
            c7h = float(f32(1.0 / 7.0))
            c7l = float(f32(np.float64(1.0 / 7.0) - np.float64(c7h)))
            ve(lambda v: v.tensor_scalar(qs[:, 1:2], qs[:, 0:1], c7h, None,
                                         ALU.mult), reads=[qs], writes=[qs])
            ve(lambda v: v.scalar_tensor_tensor(qs[:, 1:2], qs[:, 0:1], c7l,
                                                qs[:, 1:2], ALU.mult, ALU.add),
               reads=[qs], writes=[qs])
            ve(lambda v: v.reciprocal(qs[:, 2:3], qs[:, 0:1]),
               reads=[qs], writes=[qs])
            ve(lambda v: v.tensor_scalar(qs[:, 2:3], qs[:, 2:3], 7.0, None,
                                         ALU.mult), reads=[qs], writes=[qs])
            for dc in range(DC):
                sl = slice(dc * 1024, (dc + 1) * 1024)
                ve(lambda v, sl=sl: v.tensor_scalar(
                    qz[:], kw[:, sl], qs[:, 2:3], 8.5, ALU.mult, ALU.add),
                   reads=[kw, qs], writes=[qz])
                ve(lambda v: v.tensor_copy(qg[:], qz[:]), reads=[qz],
                   writes=[qg])
                for H in (8.0, 4.0, 2.0, 1.0):
                    ve(lambda v, H=H: v.tensor_scalar(
                        qt[:], qg[:], float(H), float(H), ALU.is_ge,
                        ALU.mult), reads=[qg], writes=[qt])
                    ve(lambda v: v.tensor_tensor(qg[:], qg[:], qt[:],
                                                 ALU.subtract),
                       reads=[qg, qt], writes=[qg])
                ve(lambda v: v.tensor_tensor(qz[:], qz[:], qg[:],
                                             ALU.subtract),
                   reads=[qz, qg], writes=[qz])
                ve(lambda v: v.tensor_scalar(qz[:], qz[:], 0.0, 15.0,
                                             ALU.max, ALU.min),
                   reads=[qz], writes=[qz])
                ve(lambda v: v.tensor_scalar(qz[:], qz[:], -8.0, qs[:, 1:2],
                                             ALU.add, ALU.mult),
                   reads=[qz, qs], writes=[qz])
                ve(lambda v: v.tensor_copy(k16a[:], qz[:]), reads=[qz],
                   writes=[k16a])
                ve(lambda v: v.scalar_tensor_tensor(k16b[:], k16a[:], -1.0,
                                                    qz[:], ALU.mult, ALU.add),
                   reads=[qz, k16a], writes=[k16b])
                dstore(kqh_d[dc * 128:(dc + 1) * 128, :], k16a[:],
                       reads=[k16a])
                dstore(kql_d[dc * 128:(dc + 1) * 128, :], k16b[:],
                       reads=[k16b])
            plan.barrier()
            plan.flush()

        with contextlib.ExitStack() as es2:
            wT = {p: (es2.enter_context(
                          nc.sbuf_tensor("wT" + p + "h", [128, DC * 1024], F16)),
                      es2.enter_context(
                          nc.sbuf_tensor("wT" + p + "l", [128, DC * 1024], F16)))
                  for p in "qkv"}
            with contextlib.ExitStack() as _es:
                wbuf = [_es.enter_context(
                    nc.sbuf_tensor("wbuf%d" % i, [128, 1024], F32))
                    for i in range(2)]
                wh16 = [_es.enter_context(
                    nc.sbuf_tensor("wh16%d" % i, [128, 1024], F16))
                    for i in range(2)]
                wl16 = [_es.enter_context(
                    nc.sbuf_tensor("wl16%d" % i, [128, 1024], F16))
                    for i in range(2)]
                ptr = [_es.enter_context(
                    nc.psum_tensor("ptr%d" % i, [128, 1024], F16))
                    for i in range(2)]
                it = 0
                for p in "qkv":
                    for ecw in range(DC):
                        wb, wh, wl = wbuf[it % 2], wh16[it % 2], wl16[it % 2]
                        dload(wb[:], w_d[p][ecw * 128:(ecw + 1) * 128, :],
                              writes=[wb])
                        ve(lambda v, wb=wb, wh=wh: v.tensor_copy(wh[:], wb[:]),
                           reads=[wb], writes=[wh])
                        ve(lambda v, wb=wb, wh=wh, wl=wl:
                           v.scalar_tensor_tensor(
                               wl[:], wh[:], -1.0, wb[:], ALU.mult, ALU.add),
                           reads=[wb, wh], writes=[wl])
                        for half, wsrc in ((0, wh), (1, wl)):
                            dstw = wT[p][half]
                            for dc in range(DC):
                                pt = ptr[dc % 2]
                                pe(lambda t, wsrc=wsrc, dc=dc, pt=pt:
                                   t.transpose(
                                       pt[:, 0:128],
                                       wsrc[:, dc * 128:(dc + 1) * 128],
                                       id16[:]),
                                   reads=[wsrc, id16], writes=[pt])
                                ve(lambda v, dstw=dstw, dc=dc, ecw=ecw, pt=pt:
                                   v.tensor_copy(
                                       dstw[:, dc * 1024 + ecw * 128:
                                            dc * 1024 + (ecw + 1) * 128],
                                       pt[:, 0:128]),
                                   reads=[pt], writes=[dstw])
                        it += 1
                plan.barrier()
                plan.flush()

            with contextlib.ExitStack() as _es:
                xt = [_es.enter_context(
                    nc.sbuf_tensor("xt%d" % i, [128, 4 * 1024], F32))
                    for i in range(2)]
                fmh = [_es.enter_context(
                    nc.sbuf_tensor("fmh%d" % i, [128, DC * 512], F16))
                    for i in range(2)]
                fml = [_es.enter_context(
                    nc.sbuf_tensor("fml%d" % i, [128, DC * 512], F16))
                    for i in range(2)]
                st1 = _es.enter_context(nc.sbuf_tensor("st1", [128, 8], F32))
                xc = _es.enter_context(nc.sbuf_tensor("xc", [128, 1024], F32))
                xsq = _es.enter_context(nc.sbuf_tensor("xsq", [128, 1024], F32))
                xnh = [_es.enter_context(
                    nc.sbuf_tensor("s_xnh%d" % i, [128, 1024], F16))
                    for i in range(2)]
                xnl = [_es.enter_context(
                    nc.sbuf_tensor("s_xnl%d" % i, [128, 1024], F16))
                    for i in range(2)]
                ptl = [_es.enter_context(
                    nc.psum_tensor("ptl%d" % i, [128, 1024], F16))
                    for i in range(2)]
                for sup in range(T // 512):
                    t0 = sup * 512
                    xb, fh, fl = xt[sup % 2], fmh[sup % 2], fml[sup % 2]
                    for i in range(4):
                        dload(xb[:, i * 1024:(i + 1) * 1024],
                              x_d[t0 + i * 128:t0 + (i + 1) * 128, :],
                              writes=[xb])
                    for i in range(4):
                        xh, xl = xnh[i % 2], xnl[i % 2]
                        xsl = xb[:, i * 1024:(i + 1) * 1024]
                        ve(lambda v, xsl=xsl, xb=xb: v.reduce_sum(
                            st1[:, 0:1], xsl, axis=AX.X),
                           reads=[xb], writes=[st1])
                        ve(lambda v: v.tensor_scalar(
                            st1[:, 1:2], st1[:, 0:1], float(f32(1.0 / D)),
                            None, ALU.mult), reads=[st1], writes=[st1])
                        ve(lambda v, xsl=xsl, xb=xb: v.tensor_scalar(
                            xc[:], xsl, st1[:, 1:2], None, ALU.subtract),
                           reads=[xb, st1], writes=[xc])
                        ve(lambda v: v.tensor_tensor(xsq[:], xc[:], xc[:],
                                                     ALU.mult),
                           reads=[xc], writes=[xsq])
                        ve(lambda v: v.reduce_sum(st1[:, 2:3], xsq[:],
                                                  axis=AX.X),
                           reads=[xsq], writes=[st1])
                        se(lambda s: s.activation(st1[:, 3:4], st1[:, 2:3],
                                                  ACT.Sqrt, bias=f32(1e-5),
                                                  scale=f32(1.0 / D)),
                           reads=[st1], writes=[st1])
                        ve(lambda v: v.reciprocal(st1[:, 4:5], st1[:, 3:4]),
                           reads=[st1], writes=[st1])
                        ve(lambda v: v.tensor_scalar(xc[:], xc[:],
                                                     st1[:, 4:5], None,
                                                     ALU.mult),
                           reads=[xc, st1], writes=[xc])
                        ve(lambda v, xh=xh: v.tensor_copy(xh[:], xc[:]),
                           reads=[xc], writes=[xh])
                        gp(lambda g, xh=xh, xl=xl: g.tensor_sub(
                            xl[:], xc[:], xh[:]),
                           reads=[xc, xh], writes=[xl])
                        for src16, fmdst in ((xh, fh), (xl, fl)):
                            for ec in range(DC):
                                pt = ptl[ec % 2]
                                pe(lambda t, src16=src16, ec=ec, pt=pt:
                                   t.transpose(
                                       pt[:, 0:128],
                                       src16[:, ec * 128:(ec + 1) * 128],
                                       id16[:]),
                                   reads=[src16, id16], writes=[pt])
                                ve(lambda v, fmdst=fmdst, ec=ec, i=i, pt=pt:
                                   v.tensor_copy(
                                       fmdst[:, ec * 512 + i * 128:
                                             ec * 512 + (i + 1) * 128],
                                       pt[:, 0:128]),
                                   reads=[pt], writes=[fmdst])
                    for ec in range(DC):
                        dstore(xnh_d[ec * 128:(ec + 1) * 128, t0:t0 + 512],
                               fh[:, ec * 512:(ec + 1) * 512], reads=[fh])
                        dstore(xnl_d[ec * 128:(ec + 1) * 128, t0:t0 + 512],
                               fl[:, ec * 512:(ec + 1) * 512], reads=[fl])
                plan.barrier()
                plan.flush()

            with contextlib.ExitStack() as _es:
                x16h = [_es.enter_context(
                    nc.sbuf_tensor("x16h%d" % i, [128, DC * 512], F16))
                    for i in range(2)]
                x16l = [_es.enter_context(
                    nc.sbuf_tensor("x16l%d" % i, [128, DC * 512], F16))
                    for i in range(2)]
                rr = [_es.enter_context(
                    nc.sbuf_tensor("rr%d" % i, [128, 1024], F32))
                    for i in range(2)]
                zc = _es.enter_context(nc.sbuf_tensor("zc", [128, 1024], F32))
                tg32 = [_es.enter_context(
                    nc.sbuf_tensor("tg32%d" % i, [128, 1024], F32))
                    for i in range(2)]
                tgh = [_es.enter_context(
                    nc.sbuf_tensor("tgh%d" % i, [128, 1024], F16))
                    for i in range(2)]
                tgl = [_es.enter_context(
                    nc.sbuf_tensor("tgl%d" % i, [128, 1024], F16))
                    for i in range(2)]
                vh = [_es.enter_context(
                    nc.sbuf_tensor("vh%d" % i, [128, 512], F16))
                    for i in range(2)]
                vl = [_es.enter_context(
                    nc.sbuf_tensor("vl%d" % i, [128, 512], F16))
                    for i in range(2)]
                vash = [_es.enter_context(
                    nc.sbuf_tensor("vash0", [128, 4 * 1024], F16))] * 2
                vasl = [_es.enter_context(
                    nc.sbuf_tensor("vasl0", [128, 4 * 1024], F16))] * 2
                pjs = [_es.enter_context(
                    nc.psum_tensor("pj%d" % i, [128, 512], F32))
                    for i in range(4)]
                ptv = [_es.enter_context(
                    nc.psum_tensor("ptv%d" % i, [128, 1024], F16))
                    for i in range(2)]

                def mm_proj_mk(ps, p, ec, xh, xl):
                    wh_t, wl_t = wT[p]
                    bh_t, bl_t = brow[p]

                    def mm_proj(t):
                        t.matmul(ps[:], bh_t[:1, ec * 128:(ec + 1) * 128],
                                 ones16[:1, 0:512], start=True, stop=False)
                        t.matmul(ps[:], bl_t[:1, ec * 128:(ec + 1) * 128],
                                 ones16[:1, 0:512], start=False, stop=False)
                        last = None
                        for dc in range(DC):
                            wsl = slice(dc * 1024 + ec * 128,
                                        dc * 1024 + (ec + 1) * 128)
                            xsl = slice(dc * 512, (dc + 1) * 512)
                            t.matmul(ps[:], wh_t[:, wsl], xh[:, xsl],
                                     start=False, stop=False)
                            t.matmul(ps[:], wh_t[:, wsl], xl[:, xsl],
                                     start=False, stop=False)
                            last = t.matmul(ps[:], wl_t[:, wsl], xh[:, xsl],
                                            start=False, stop=dc == DC - 1)
                        return last
                    return mm_proj, [wh_t, wl_t, bh_t, bl_t, xh, xl, ones16]

                trig_it = 0
                for blk in range(T // 512):
                    t0 = blk * 512
                    xh, xl = x16h[blk % 2], x16l[blk % 2]
                    for ec in range(DC):
                        dload(xh[:, ec * 512:(ec + 1) * 512],
                              xnh_d[ec * 128:(ec + 1) * 128, t0:t0 + 512],
                              writes=[xh])
                        dload(xl[:, ec * 512:(ec + 1) * 512],
                              xnl_d[ec * 128:(ec + 1) * 128, t0:t0 + 512],
                              writes=[xl])
                    vas_h, vas_l = vash[blk % 2], vasl[blk % 2]
                    for ec in range(DC):
                        psq = pjs[(3 * ec) % 4]
                        psk = pjs[(3 * ec + 1) % 4]
                        psv = pjs[(3 * ec + 2) % 4]
                        emq, rdq = mm_proj_mk(psq, "q", ec, xh, xl)
                        pe(emq, reads=rdq, writes=[psq])
                        emk, rdk = mm_proj_mk(psk, "k", ec, xh, xl)
                        pe(emk, reads=rdk, writes=[psk])
                        emv, rdv = mm_proj_mk(psv, "v", ec, xh, xl)
                        pe(emv, reads=rdv, writes=[psv])
                        rb = rr[trig_it % 2]
                        ve(lambda v, psq=psq, ec=ec, rb=rb: v.tensor_scalar(
                            rb[:, 0:512], psq[:], chi[:, ec:ec + 1],
                            pbz[:, ec:ec + 1], ALU.mult, ALU.add),
                           reads=[psq, chi, pbz], writes=[rb])
                        ve(lambda v, psq=psq, ec=ec, rb=rb:
                           v.scalar_tensor_tensor(
                               rb[:, 0:512], psq[:], clo[:, ec:ec + 1],
                               rb[:, 0:512], ALU.mult, ALU.add),
                           reads=[psq, clo, rb], writes=[rb])
                        ve(lambda v, psk=psk, ec=ec, rb=rb: v.tensor_scalar(
                            rb[:, 512:1024], psk[:], chi[:, ec:ec + 1],
                            pbz[:, ec:ec + 1], ALU.mult, ALU.add),
                           reads=[psk, chi, pbz], writes=[rb])
                        ve(lambda v, psk=psk, ec=ec, rb=rb:
                           v.scalar_tensor_tensor(
                               rb[:, 512:1024], psk[:], clo[:, ec:ec + 1],
                               rb[:, 512:1024], ALU.mult, ALU.add),
                           reads=[psk, clo, rb], writes=[rb])
                        vhb, vlb = vh[ec % 2], vl[ec % 2]
                        ve(lambda v, psv=psv, vhb=vhb: v.tensor_copy(
                            vhb[:], psv[:]), reads=[psv], writes=[vhb])
                        ve(lambda v, psv=psv, vhb=vhb, vlb=vlb:
                           v.scalar_tensor_tensor(
                               vlb[:], vhb[:], -1.0, psv[:], ALU.mult,
                               ALU.add), reads=[psv, vhb], writes=[vlb])
                        for vsrc, vdst in ((vhb, vas_h), (vlb, vas_l)):
                            for tc4 in range(4):
                                pt = ptv[tc4 % 2]
                                pe(lambda t, vsrc=vsrc, tc4=tc4, pt=pt:
                                   t.transpose(
                                       pt[:, 0:128],
                                       vsrc[:, tc4 * 128:(tc4 + 1) * 128],
                                       id16[:]),
                                   reads=[vsrc, id16], writes=[pt])
                                ve(lambda v, vdst=vdst, tc4=tc4, ec=ec, pt=pt:
                                   v.tensor_copy(
                                       vdst[:, tc4 * 1024 + ec * 128:
                                            tc4 * 1024 + (ec + 1) * 128],
                                       pt[:, 0:128]),
                                   reads=[pt], writes=[vdst])
                        m = M_FOLDS[ec]
                        for fn, shift in (("c", 0.0), ("s", 0.25)):
                            tg = tg32[trig_it % 2]
                            th_, tl_ = tgh[trig_it % 2], tgl[trig_it % 2]
                            se(lambda s, rb=rb, shift=shift: s.activation(
                                zc[:], rb[:], ACT.Abs,
                                bias=f32(-shift) if shift else f32(0.0)),
                               reads=[rb], writes=[zc])
                            for j in range(m - 1, -1, -1):
                                se(lambda s, j=j: s.activation(
                                    zc[:], zc[:], ACT.Abs,
                                    bias=f32(-(2.0 ** j))),
                                   reads=[zc], writes=[zc])
                            se(lambda s: s.activation(zc[:], zc[:], ACT.Abs,
                                                      bias=f32(-0.5)),
                               reads=[zc], writes=[zc])
                            se(lambda s, tg=tg: s.activation(
                                tg[:], zc[:], ACT.Sin, bias=f32(-PI / 2),
                                scale=f32(TWO_PI)),
                               reads=[zc], writes=[tg])
                            gp(lambda g, tg=tg, th_=th_: g.tensor_copy(
                                th_[:], tg[:]), reads=[tg], writes=[th_])
                            gp(lambda g, tg=tg, th_=th_, tl_=tl_:
                               g.tensor_sub(tl_[:], tg[:], th_[:]),
                               reads=[tg, th_], writes=[tl_])
                            qnm = "cq" if fn == "c" else "sq"
                            knm = "ck" if fn == "c" else "sk"
                            esl = slice(ec * 128, (ec + 1) * 128)
                            dstore(feat_d[qnm][0][esl, t0:t0 + 512],
                                   th_[:, 0:512], reads=[th_])
                            dstore(feat_d[knm][0][esl, t0:t0 + 512],
                                   th_[:, 512:1024], reads=[th_])
                            dstore(feat_d[qnm][1][esl, t0:t0 + 512],
                                   tl_[:, 0:512], reads=[tl_])
                            dstore(feat_d[knm][1][esl, t0:t0 + 512],
                                   tl_[:, 512:1024], reads=[tl_])
                            trig_it += 1
                    for tc4 in range(4):
                        dstore(vth_d[t0 + tc4 * 128:t0 + (tc4 + 1) * 128, :],
                               vas_h[:, tc4 * 1024:(tc4 + 1) * 1024],
                               reads=[vas_h])
                        dstore(vtl_d[t0 + tc4 * 128:t0 + (tc4 + 1) * 128, :],
                               vas_l[:, tc4 * 1024:(tc4 + 1) * 1024],
                               reads=[vas_l])
                plan.barrier()
                plan.flush()

        with contextlib.ExitStack() as _es:
            cqh_s = _es.enter_context(nc.sbuf_tensor("cqh_s", [128, DC * QQ], F16))
            cql_s = _es.enter_context(nc.sbuf_tensor("cql_s", [128, DC * QQ], F16))
            sqh_s = _es.enter_context(nc.sbuf_tensor("sqh_s", [128, DC * QQ], F16))
            sql_s = _es.enter_context(nc.sbuf_tensor("sql_s", [128, DC * QQ], F16))
            ckh2 = _es.enter_context(nc.sbuf_tensor("ckh_s", [128, DC * TB], F16))
            ckl2 = _es.enter_context(nc.sbuf_tensor("ckl_s", [128, DC * TB], F16))
            skh2 = _es.enter_context(nc.sbuf_tensor("skh_s", [128, DC * TB], F16))
            skl2 = _es.enter_context(nc.sbuf_tensor("skl_s", [128, DC * TB], F16))
            simr = _es.enter_context(
                nc.sbuf_tensor("simr", [128, 4 * 2048], F32))
            prob = _es.enter_context(nc.sbuf_tensor("prob", [128, 2048], F32))
            ph1 = _es.enter_context(nc.sbuf_tensor("ph1", [128, 2048], F16))
            pl1 = _es.enter_context(nc.sbuf_tensor("pl1", [128, 2048], F16))
            ptTh = _es.enter_context(
                nc.sbuf_tensor("ptTh", [128, NSC * QQ], F16))
            ptTl = _es.enter_context(
                nc.sbuf_tensor("ptTl", [128, NSC * QQ], F16))
            vb_h = _es.enter_context(nc.sbuf_tensor("vbh", [128, TC * 1024], F16))
            vb_l = _es.enter_context(nc.sbuf_tensor("vbl", [128, TC * 1024], F16))
            smax = _es.enter_context(nc.sbuf_tensor("smax", [128, 8], F32))
            aout = [_es.enter_context(
                nc.sbuf_tensor("aout%d" % i, [128, 1024], F32))
                for i in range(2)]
            pas = [_es.enter_context(
                nc.psum_tensor("pa%d" % i, [128, 512], F32))
                for i in range(4)]
            ptp = [_es.enter_context(
                nc.psum_tensor("ptp%d" % i, [128, 1024], F16))
                for i in range(2)]
            assert SB <= 4 and S <= 2048
            qarr = {"cq": (cqh_s, cql_s), "sq": (sqh_s, sql_s)}
            for seq in range(SEQS):
                sbase = seq * S
                for qq in range(NQQ):
                    q0 = sbase + qq * QQ
                    for nm in ("cq", "sq"):
                        bh2, bl2 = qarr[nm]
                        sh_d, sl_d = feat_d[nm]
                        for ec in range(DC):
                            dload(bh2[:, ec * QQ:(ec + 1) * QQ],
                                  sh_d[ec * 128:(ec + 1) * 128, q0:q0 + QQ],
                                  writes=[bh2])
                            dload(bl2[:, ec * QQ:(ec + 1) * QQ],
                                  sl_d[ec * 128:(ec + 1) * 128, q0:q0 + QQ],
                                  writes=[bl2])
                    for tb in range(NTB):
                        k0 = sbase + tb * TB
                        for ec in range(DC):
                            dload(ckh2[:, ec * TB:(ec + 1) * TB],
                                  feat_d["ck"][0][ec * 128:(ec + 1) * 128,
                                                  k0:k0 + TB], writes=[ckh2])
                            dload(ckl2[:, ec * TB:(ec + 1) * TB],
                                  feat_d["ck"][1][ec * 128:(ec + 1) * 128,
                                                  k0:k0 + TB], writes=[ckl2])
                        for sbi in range(SB):
                            ps = pas[sbi % 4]

                            def mm_cos(t, ps=ps, sbi=sbi):
                                qh2, ql2 = qarr["cq"]
                                last = None
                                for ec in range(DC):
                                    qsl = slice(ec * QQ + sbi * 128,
                                                ec * QQ + (sbi + 1) * 128)
                                    ksl = slice(ec * TB, (ec + 1) * TB)
                                    t.matmul(ps[:, 0:TB], qh2[:, qsl],
                                             ckh2[:, ksl], start=ec == 0,
                                             stop=False)
                                    t.matmul(ps[:, 0:TB], qh2[:, qsl],
                                             ckl2[:, ksl], start=False,
                                             stop=False)
                                    last = t.matmul(ps[:, 0:TB], ql2[:, qsl],
                                                    ckh2[:, ksl],
                                                    start=False, stop=False)
                                return last

                            pe(mm_cos, reads=[cqh_s, cql_s, ckh2, ckl2],
                               writes=[ps])
                        for ec in range(DC):
                            dload(skh2[:, ec * TB:(ec + 1) * TB],
                                  feat_d["sk"][0][ec * 128:(ec + 1) * 128,
                                                  k0:k0 + TB], writes=[skh2])
                            dload(skl2[:, ec * TB:(ec + 1) * TB],
                                  feat_d["sk"][1][ec * 128:(ec + 1) * 128,
                                                  k0:k0 + TB], writes=[skl2])
                        for sbi in range(SB):
                            ps = pas[sbi % 4]

                            def mm_sin(t, ps=ps, sbi=sbi):
                                qh2, ql2 = qarr["sq"]
                                last = None
                                for ec in range(DC):
                                    qsl = slice(ec * QQ + sbi * 128,
                                                ec * QQ + (sbi + 1) * 128)
                                    ksl = slice(ec * TB, (ec + 1) * TB)
                                    t.matmul(ps[:, 0:TB], qh2[:, qsl],
                                             skh2[:, ksl], start=False,
                                             stop=False)
                                    t.matmul(ps[:, 0:TB], qh2[:, qsl],
                                             skl2[:, ksl], start=False,
                                             stop=False)
                                    last = t.matmul(ps[:, 0:TB], ql2[:, qsl],
                                                    skh2[:, ksl],
                                                    start=False,
                                                    stop=ec == DC - 1)
                                return last

                            pe(mm_sin, reads=[sqh_s, sql_s, skh2, skl2],
                               writes=[ps])
                            se(lambda s, ps=ps, sbi=sbi, tb=tb: s.activation(
                                simr[:, sbi * S + tb * TB:
                                     sbi * S + (tb + 1) * TB],
                                ps[:, 0:TB], ACT.Copy),
                               reads=[ps], writes=[simr])
                    for sbi in range(SB):
                        ssl = slice(sbi * S, (sbi + 1) * S)
                        ve(lambda v, ssl=ssl, sbi=sbi: v.reduce_max(
                            smax[:, sbi:sbi + 1], simr[:, ssl], axis=AX.X,
                            negate=True), reads=[simr], writes=[smax])
                        ve(lambda v, sbi=sbi: v.tensor_scalar(
                            smax[:, 4 + sbi:5 + sbi], smax[:, sbi:sbi + 1],
                            float(f32(1.0 / 32.0)), None, ALU.mult),
                           reads=[smax], writes=[smax])
                        se(lambda s, ssl=ssl, sbi=sbi: s.activation(
                            prob[:, 0:S], simr[:, ssl], ACT.Exp,
                            bias=smax[:, 4 + sbi:5 + sbi],
                            scale=f32(1.0 / 32.0)),
                           reads=[simr, smax], writes=[prob])
                        ve(lambda v, sbi=sbi: v.reduce_sum(
                            smax[:, sbi:sbi + 1], prob[:, 0:S], axis=AX.X),
                           reads=[prob], writes=[smax])
                        ve(lambda v, sbi=sbi: v.reciprocal(
                            smax[:, 4 + sbi:5 + sbi], smax[:, sbi:sbi + 1]),
                           reads=[smax], writes=[smax])
                        ve(lambda v: v.tensor_copy(ph1[:, 0:S], prob[:, 0:S]),
                           reads=[prob], writes=[ph1])
                        gp(lambda g: g.tensor_sub(pl1[:, 0:S], prob[:, 0:S],
                                                  ph1[:, 0:S]),
                           reads=[prob, ph1], writes=[pl1])
                        for tcg in range(NSC):
                            for src1, dstT in ((ph1, ptTh), (pl1, ptTl)):
                                pt = ptp[tcg % 2]
                                pe(lambda t, src1=src1, tcg=tcg, pt=pt:
                                   t.transpose(
                                       pt[:, 0:128],
                                       src1[:, tcg * 128:(tcg + 1) * 128],
                                       id16[:]),
                                   reads=[src1, id16], writes=[pt])
                                ve(lambda v, dstT=dstT, tcg=tcg, sbi=sbi,
                                   pt=pt: v.tensor_copy(
                                       dstT[:, tcg * QQ + sbi * 128:
                                            tcg * QQ + (sbi + 1) * 128],
                                       pt[:, 0:128]),
                                   reads=[pt], writes=[dstT])
                    for pair in range(SB // 2):
                        sb0 = pair * 2
                        for tb in range(NTB):
                            k0r = sbase + tb * TB
                            for tci in range(TC):
                                dload(vb_h[:, tci * 1024:(tci + 1) * 1024],
                                      vth_d[k0r + tci * 128:
                                            k0r + (tci + 1) * 128, :],
                                      writes=[vb_h])
                            for sbi in range(sb0, sb0 + 2):
                                for dh in range(2):
                                    pv = pas[(sbi - sb0) * 2 + dh]
                                    strt = tb == 0

                                    def mm_pvh(t, pv=pv, sbi=sbi, dh=dh,
                                               tb=tb, strt=strt):
                                        last = None
                                        for tci in range(TC):
                                            tcg = tb * TC + tci
                                            tsl = slice(
                                                tcg * QQ + sbi * 128,
                                                tcg * QQ + (sbi + 1) * 128)
                                            vsl = slice(
                                                tci * 1024 + dh * 512,
                                                tci * 1024 + (dh + 1) * 512)
                                            t.matmul(pv[:], ptTh[:, tsl],
                                                     vb_h[:, vsl],
                                                     start=strt and tci == 0,
                                                     stop=False)
                                            last = t.matmul(
                                                pv[:], ptTl[:, tsl],
                                                vb_h[:, vsl], start=False,
                                                stop=False)
                                        return last

                                    pe(mm_pvh, reads=[ptTh, ptTl, vb_h],
                                       writes=[pv])
                            for tci in range(TC):
                                dload(vb_l[:, tci * 1024:(tci + 1) * 1024],
                                      vtl_d[k0r + tci * 128:
                                            k0r + (tci + 1) * 128, :],
                                      writes=[vb_l])
                            for sbi in range(sb0, sb0 + 2):
                                for dh in range(2):
                                    pv = pas[(sbi - sb0) * 2 + dh]
                                    stp = tb == NTB - 1

                                    def mm_pvl(t, pv=pv, sbi=sbi, dh=dh,
                                               tb=tb, stp=stp):
                                        last = None
                                        for tci in range(TC):
                                            tcg = tb * TC + tci
                                            tsl = slice(
                                                tcg * QQ + sbi * 128,
                                                tcg * QQ + (sbi + 1) * 128)
                                            vsl = slice(
                                                tci * 1024 + dh * 512,
                                                tci * 1024 + (dh + 1) * 512)
                                            last = t.matmul(
                                                pv[:], ptTh[:, tsl],
                                                vb_l[:, vsl], start=False,
                                                stop=(stp and
                                                      tci == TC - 1))
                                        return last

                                    pe(mm_pvl, reads=[ptTh, vb_l],
                                       writes=[pv])
                        for sbi in range(sb0, sb0 + 2):
                            ao = aout[sbi % 2]
                            for dh in range(2):
                                pv = pas[(sbi - sb0) * 2 + dh]
                                ve(lambda v, pv=pv, dh=dh, sbi=sbi, ao=ao:
                                   v.tensor_scalar(
                                       ao[:, dh * 512:(dh + 1) * 512],
                                       pv[:], smax[:, 4 + sbi:5 + sbi], None,
                                       ALU.mult),
                                   reads=[pv, smax], writes=[ao])
                            dstore(attn_d[q0 + sbi * 128:
                                          q0 + (sbi + 1) * 128, :],
                                   ao[:], reads=[ao])
            plan.barrier()
            plan.flush()

        with contextlib.ExitStack() as _es:
            k16h_s = _es.enter_context(
                nc.sbuf_tensor("k16h_s", [128, DC * 1024], F16))
            k16l_s = _es.enter_context(
                nc.sbuf_tensor("k16l_s", [128, DC * 1024], F16))
            x16h4 = [_es.enter_context(
                nc.sbuf_tensor("x16h4%d" % i, [128, DC * 512], F16))
                for i in range(2)]
            x16l4 = [_es.enter_context(
                nc.sbuf_tensor("x16l4%d" % i, [128, DC * 512], F16))
                for i in range(2)]
            xres = [_es.enter_context(
                nc.sbuf_tensor("xres%d" % i, [128, 4 * 1024], F32))
                for i in range(2)]
            ares = [_es.enter_context(
                nc.sbuf_tensor("ares%d" % i, [128, 4 * 1024], F32))
                for i in range(2)]
            t1h = [_es.enter_context(
                nc.sbuf_tensor("t1h%d" % i, [16, 512], F16))
                for i in range(2)]
            t1l = [_es.enter_context(
                nc.sbuf_tensor("t1l%d" % i, [16, 512], F16))
                for i in range(2)]
            oacc = [_es.enter_context(
                nc.sbuf_tensor("oacc%d" % i, [128, 1024], F32))
                for i in range(2)]
            pfs = [_es.enter_context(
                nc.psum_tensor("pf%d" % i, [128, 512], F32))
                for i in range(4)]
            pt1 = [_es.enter_context(
                nc.psum_tensor("pt1%d" % i, [128, 512], F32))
                for i in range(2)]
            for dc in range(DC):
                dload(k16h_s[:, dc * 1024:(dc + 1) * 1024],
                      kqh_d[dc * 128:(dc + 1) * 128, :], writes=[k16h_s])
                dload(k16l_s[:, dc * 1024:(dc + 1) * 1024],
                      kql_d[dc * 128:(dc + 1) * 128, :], writes=[k16l_s])
            for blk in range(T // 512):
                t0 = blk * 512
                xh4, xl4 = x16h4[blk % 2], x16l4[blk % 2]
                xr, ar = xres[blk % 2], ares[blk % 2]
                for ec in range(DC):
                    dload(xh4[:, ec * 512:(ec + 1) * 512],
                          xnh_d[ec * 128:(ec + 1) * 128, t0:t0 + 512],
                          writes=[xh4])
                    dload(xl4[:, ec * 512:(ec + 1) * 512],
                          xnl_d[ec * 128:(ec + 1) * 128, t0:t0 + 512],
                          writes=[xl4])
                for i in range(4):
                    dload(xr[:, i * 1024:(i + 1) * 1024],
                          x_d[t0 + i * 128:t0 + (i + 1) * 128, :],
                          writes=[xr])
                    dload(ar[:, i * 1024:(i + 1) * 1024],
                          attn_d[t0 + i * 128:t0 + (i + 1) * 128, :],
                          writes=[ar])
                th1, tl1 = t1h[blk % 2], t1l[blk % 2]
                p1 = pt1[blk % 2]

                def mm_t1(t, xh4=xh4, xl4=xl4, p1=p1):
                    last = None
                    for dc in range(DC):
                        asl = slice(dc * 16, (dc + 1) * 16)
                        xsl = slice(dc * 512, (dc + 1) * 512)
                        t.matmul(p1[:16, :], a16h[:, asl], xh4[:, xsl],
                                 start=dc == 0, stop=False)
                        t.matmul(p1[:16, :], a16h[:, asl], xl4[:, xsl],
                                 start=False, stop=False)
                        last = t.matmul(p1[:16, :], a16l[:, asl],
                                        xh4[:, xsl], start=False,
                                        stop=dc == DC - 1)
                    return last

                pe(mm_t1, reads=[a16h, a16l, xh4, xl4], writes=[p1])
                ve(lambda v, th1=th1, p1=p1: v.tensor_copy(th1[:16, :],
                                                           p1[:16, :]),
                   reads=[p1], writes=[th1])
                ve(lambda v, th1=th1, tl1=tl1, p1=p1: v.scalar_tensor_tensor(
                    tl1[:16, :], th1[:16, :], -1.0, p1[:16, :], ALU.mult,
                    ALU.add), reads=[p1, th1], writes=[tl1])
                for i in range(4):
                    oa = oacc[i % 2]
                    for dh in range(2):
                        pf = pfs[(i % 2) * 2 + dh]
                        fsl = slice(dh * 512, (dh + 1) * 512)

                        def mm_ffn(t, pf=pf, fsl=fsl, i=i, xh4=xh4, xl4=xl4,
                                   th1=th1, tl1=tl1):
                            fh, fl = brow["f"]
                            t.matmul(pf[:], ones16[:1, 0:128], fh[:1, fsl],
                                     start=True, stop=False)
                            t.matmul(pf[:], ones16[:1, 0:128], fl[:1, fsl],
                                     start=False, stop=False)
                            for dc in range(DC):
                                ksl = slice(dc * 1024 + fsl.start,
                                            dc * 1024 + fsl.stop)
                                xsl = slice(dc * 512 + i * 128,
                                            dc * 512 + (i + 1) * 128)
                                t.matmul(pf[:], xh4[:, xsl], k16h_s[:, ksl],
                                         start=False, stop=False)
                                t.matmul(pf[:], xh4[:, xsl], k16l_s[:, ksl],
                                         start=False, stop=False)
                                t.matmul(pf[:], xl4[:, xsl], k16h_s[:, ksl],
                                         start=False, stop=False)
                            tsl = slice(i * 128, (i + 1) * 128)
                            t.matmul(pf[:], th1[:16, tsl], b16h[:16, fsl],
                                     start=False, stop=False)
                            t.matmul(pf[:], th1[:16, tsl], b16l[:16, fsl],
                                     start=False, stop=False)
                            return t.matmul(pf[:], tl1[:16, tsl],
                                            b16h[:16, fsl], start=False,
                                            stop=True)

                        pe(mm_ffn, reads=[xh4, xl4, k16h_s, k16l_s, th1,
                                          tl1, b16h, b16l, ones16,
                                          brow["f"][0], brow["f"][1]],
                           writes=[pf])
                    ve(lambda v, i=i, oa=oa, xr=xr, ar=ar: v.tensor_tensor(
                        oa[:], xr[:, i * 1024:(i + 1) * 1024],
                        ar[:, i * 1024:(i + 1) * 1024], ALU.add),
                       reads=[xr, ar], writes=[oa])
                    ve(lambda v, oa=oa, pf=pfs[(i % 2) * 2]: v.tensor_tensor(
                        oa[:, 0:512], oa[:, 0:512], pf[:], ALU.add),
                       reads=[pfs[(i % 2) * 2], oa], writes=[oa])
                    ve(lambda v, oa=oa, pf=pfs[(i % 2) * 2 + 1]:
                       v.tensor_tensor(
                           oa[:, 512:1024], oa[:, 512:1024], pf[:], ALU.add),
                       reads=[pfs[(i % 2) * 2 + 1], oa], writes=[oa])
                    dstore(out_d[t0 + i * 128:t0 + (i + 1) * 128, :],
                           oa[:], reads=[oa])
            plan.barrier()
            plan.flush()
    return nc


_NC_CACHE = {}


def _get_nc(key):
    if key not in _NC_CACHE:
        _NC_CACHE[key] = build_nc(*key)
    return _NC_CACHE[key]


def make_in_maps(L0_attn_x, L0_attn_Wq, L0_attn_bq, L0_attn_Wk, L0_attn_bk,
                 L0_attn_Wv, L0_attn_bv, L0_attn_phase_bias, L0_ffn_kernel,
                 L0_ffn_bias, L0_ffn_a, L0_ffn_b, n_cores=N_CORES):
    x = np.ascontiguousarray(np.asarray(L0_attn_x, dtype=np.float32))
    B, S, _ = x.shape
    SEQS = B // n_cores
    common = {
        "wq": np.ascontiguousarray(np.asarray(L0_attn_Wq, np.float32)),
        "wk": np.ascontiguousarray(np.asarray(L0_attn_Wk, np.float32)),
        "wv": np.ascontiguousarray(np.asarray(L0_attn_Wv, np.float32)),
        "bq": np.asarray(L0_attn_bq, np.float32).reshape(1, D),
        "bk": np.asarray(L0_attn_bk, np.float32).reshape(1, D),
        "bv": np.asarray(L0_attn_bv, np.float32).reshape(1, D),
        "pb_pp": _pp(np.asarray(L0_attn_phase_bias, np.float32)),
        "kern": np.ascontiguousarray(np.asarray(L0_ffn_kernel, np.float32)),
        "fb": np.asarray(L0_ffn_bias, np.float32).reshape(1, D),
        "a": np.ascontiguousarray(np.asarray(L0_ffn_a, np.float32)),
        "bl": np.ascontiguousarray(np.asarray(L0_ffn_b, np.float32)),
        "chi": _pp(C2PI_HI),
        "clo": _pp(C2PI_LO),
        "id16": np.eye(128, dtype=np.float16),
        "idf": np.eye(128, dtype=np.float32),
    }
    in_maps = []
    for c in range(n_cores):
        m = dict(common)
        m["x"] = np.ascontiguousarray(
            x[c * SEQS:(c + 1) * SEQS].reshape(SEQS * S, D))
        in_maps.append(m)
    return in_maps, SEQS, S


def kernel(**inputs):
    in_maps, SEQS, S = make_in_maps(**inputs)
    nc = _get_nc((SEQS, S, 512, 512))
    res = run_bass_kernel_spmd(nc, in_maps, core_ids=list(range(N_CORES)))
    outs = [np.asarray(r["out"]).reshape(SEQS, S, D) for r in res.results]
    return np.concatenate(outs, axis=0)


# version 04
# speedup vs baseline: 1.6265x; 1.0032x over previous
import math
import contextlib
import numpy as np
import concourse.bass as bass
import concourse.mybir as mybir
from concourse.bass_utils import run_bass_kernel_spmd

F32 = mybir.dt.float32
F16 = mybir.dt.float16
ALU = mybir.AluOpType
ACT = mybir.ActivationFunctionType
AX = mybir.AxisListType

f32 = np.float32
PI = math.pi
TWO_PI = 2.0 * math.pi

D = 1024
DC = D // 128
N_CORES = 8


def _host_consts():
    wl = np.arange(1, D + 1, dtype=np.float32) * f32(TWO_PI / D)
    inv_wl = (f32(1.0) / (wl + f32(1e-8))).astype(np.float32)
    c = inv_wl.astype(np.float64) / TWO_PI
    c_hi = c.astype(np.float32)
    c_lo = (c - c_hi.astype(np.float64)).astype(np.float32)
    QB, PBM = 48.0, 3.2
    m_folds = []
    for ec in range(DC):
        rr = (QB * float(inv_wl[ec * 128:(ec + 1) * 128].max()) + PBM) / TWO_PI
        m_folds.append(max(0, int(math.ceil(math.log2(max(rr, 1.0))))))
    return inv_wl, c_hi, c_lo, m_folds


INV_WL, C2PI_HI, C2PI_LO, M_FOLDS = _host_consts()


def _pp(v):
    return np.ascontiguousarray(v.reshape(DC, 128).T)


ENGS = ("sync", "vector", "scalar", "tensor", "gpsimd")


class Ctr:
    def __init__(self, es, nc, name):
        self.sem = es.enter_context(nc.semaphore(name))
        self.v = 0


class Plan:

    def __init__(self, es, nc, block):
        self.nc = nc
        self.es = es
        self.block = block
        self.steps = {e: [] for e in ENGS}
        self.c = {e: Ctr(es, nc, "c_" + e) for e in ENGS}
        self.all_ctrs = list(self.c.values())
        self.buf = {}
        self.dma_ctrs = {}
        self.last_wait = {e: {} for e in ENGS}

    def dma_ctr(self, buf):
        k = id(buf)
        if k not in self.dma_ctrs:
            ctr = Ctr(self.es, self.nc, "d%d" % len(self.dma_ctrs))
            self.dma_ctrs[k] = ctr
            self.all_ctrs.append(ctr)
        return self.dma_ctrs[k]

    def _dep(self, reads, writes):
        waits = {}
        for b in reads:
            st = self.buf.get(id(b))
            if st and st["ready"]:
                c, v = st["ready"]
                waits[c] = max(waits.get(c, 0), v)
        for b in writes:
            st = self.buf.get(id(b))
            if st:
                if st["ready"]:
                    c, v = st["ready"]
                    waits[c] = max(waits.get(c, 0), v)
                for c, v in st["cons"].items():
                    waits[c] = max(waits.get(c, 0), v)
        return waits

    def add(self, eng, emit, reads=(), writes=(), extra_waits=(), dma=0,
            nop=False, dma_buf=None):
        ctr = self.dma_ctr(dma_buf) if dma_buf is not None else self.c[eng]
        waits = self._dep(reads, writes)
        for (c2, v2) in extra_waits:
            waits[c2] = max(waits.get(c2, 0), v2)
        n = 16 * dma if dma else 1
        token = (ctr, ctr.v + n)
        self.steps[eng].append(
            {"emit": emit, "waits": waits, "ctr": ctr, "dma": dma, "nop": nop,
             "waitonly": False})
        ctr.v += n
        for b in reads:
            st = self.buf.setdefault(id(b), {"ready": None, "cons": {}})
            st["cons"][token[0]] = max(st["cons"].get(token[0], 0), token[1])
        for b in writes:
            self.buf[id(b)] = {"ready": token, "cons": {}}
        return token

    def barrier(self):
        vals = [(c, c.v) for c in self.all_ctrs]
        for eng in ENGS:
            ew = [(c, v) for c, v in vals if c is not self.c[eng]]
            waits = {}
            for (c2, v2) in ew:
                waits[c2] = max(waits.get(c2, 0), v2)
            self.steps[eng].append({"emit": None, "waits": waits,
                                    "ctr": self.c[eng], "dma": 0,
                                    "nop": False, "waitonly": True})

    def flush(self):
        for eng in ENGS:
            steps = self.steps[eng]
            if not steps:
                continue
            self.steps[eng] = []
            my = self.c[eng]
            lw = self.last_wait[eng]

            def section(engine, steps=steps, my=my, lw=lw):
                for st in steps:
                    for c, v in sorted(st["waits"].items(),
                                       key=lambda kv: id(kv[0])):
                        if lw.get(id(c), -1) < v:
                            engine.wait_ge(c.sem, v)
                            lw[id(c)] = v
                    if st["waitonly"]:
                        pass
                    elif st["nop"]:
                        engine.nop().then_inc(st["ctr"].sem, 1)
                    elif st["dma"]:
                        st["emit"](engine, st["ctr"].sem)
                    else:
                        st["emit"](engine).then_inc(st["ctr"].sem, 1)

            getattr(self.block, eng)(section)


def build_nc(SEQS=4, S=2048, QQ=512, TB=512, dbg=False):
    T = SEQS * S
    SB = QQ // 128
    NQQ = S // QQ
    NTB = S // TB
    TC = TB // 128
    NSC = S // 128
    assert SB % 2 == 0 and S % QQ == 0 and S % TB == 0
    nc = bass.Bass("TRN2", target_bir_lowering=False)

    x_d = nc.dram_tensor("x", [T, D], F32, kind="ExternalInput").ap()
    w_d = {p: nc.dram_tensor("w" + p, [D, D], F32, kind="ExternalInput").ap()
           for p in "qkv"}
    b_d = {p: nc.dram_tensor("b" + p, [1, D], F32, kind="ExternalInput").ap()
           for p in "qkv"}
    pb_pp_d = nc.dram_tensor("pb_pp", [128, DC], F32, kind="ExternalInput").ap()
    kern_d = nc.dram_tensor("kern", [D, D], F32, kind="ExternalInput").ap()
    fb_d = nc.dram_tensor("fb", [1, D], F32, kind="ExternalInput").ap()
    a_d = nc.dram_tensor("a", [D, 16], F32, kind="ExternalInput").ap()
    bl_d = nc.dram_tensor("bl", [16, D], F32, kind="ExternalInput").ap()
    chi_d = nc.dram_tensor("chi", [128, DC], F32, kind="ExternalInput").ap()
    clo_d = nc.dram_tensor("clo", [128, DC], F32, kind="ExternalInput").ap()
    id16_d = nc.dram_tensor("id16", [128, 128], F16, kind="ExternalInput").ap()
    idf_d = nc.dram_tensor("idf", [128, 128], F32, kind="ExternalInput").ap()
    out_d = nc.dram_tensor("out", [T, D], F32, kind="ExternalOutput").ap()

    def scr(name, shape, dt):
        return nc.dram_tensor(name, shape, dt,
                              kind="ExternalOutput" if dbg else "Internal").ap()

    xnh_d, xnl_d = scr("xnh", [D, T], F16), scr("xnl", [D, T], F16)
    feat_d = {nm: (scr(nm + "h", [D, T], F16), scr(nm + "l", [D, T], F16))
              for nm in ("cq", "sq", "ck", "sk")}
    vth_d, vtl_d = scr("vth", [T, D], F16), scr("vtl", [T, D], F16)
    attn_d = scr("attn", [T, D], F32)
    kqh_d, kql_d = scr("kqh", [D, D], F16), scr("kql", [D, D], F16)

    es = contextlib.ExitStack()
    with es:
        def sb(name, shape, dt):
            return es.enter_context(nc.sbuf_tensor(name, shape, dt))

        id16 = sb("s_id16", [128, 128], F16)
        idf = sb("s_idf", [128, 128], F32)
        chi = sb("s_chi", [128, DC], F32)
        clo = sb("s_clo", [128, DC], F32)
        pbz = sb("pbz", [128, DC], F32)
        ones16 = sb("ones16", [1, 1024], F16)
        onesf = sb("onesf", [1, 128], F32)
        brow = {p: (sb("b" + p + "hr", [1, D], F16),
                    sb("b" + p + "lr", [1, D], F16))
                for p in ("q", "k", "v", "f")}
        a16h = sb("a16h", [128, DC * 16], F16)
        a16l = sb("a16l", [128, DC * 16], F16)
        b16h = sb("b16h", [16, D], F16)
        b16l = sb("b16l", [16, D], F16)

        block = es.enter_context(nc.Block())
        plan = Plan(es, nc, block)

        def dload(dst_ap, src_ap, reads=(), writes=()):
            assert len(writes) == 1
            def emit(engine, sem, dst=dst_ap, src=src_ap):
                engine.dma_start(out=dst, in_=src).then_inc(sem, 16)
            return plan.add("sync", emit, reads=reads, writes=writes, dma=1,
                            dma_buf=writes[0])

        def dstore(dst_ap, src_ap, reads=(), writes=()):
            assert len(reads) == 1
            def emit(engine, sem, dst=dst_ap, src=src_ap):
                engine.dma_start(out=dst, in_=src).then_inc(sem, 16)
            return plan.add("gpsimd", emit, reads=reads, writes=writes, dma=1,
                            dma_buf=reads[0])

        ve = lambda emit, **kw: plan.add("vector", emit, **kw)
        se = lambda emit, **kw: plan.add("scalar", emit, **kw)
        pe = lambda emit, **kw: plan.add("tensor", emit, **kw)
        gp = lambda emit, **kw: plan.add("gpsimd", emit, **kw)

        dload(id16[:], id16_d, writes=[id16])
        dload(idf[:], idf_d, writes=[idf])
        dload(chi[:], chi_d, writes=[chi])
        dload(clo[:], clo_d, writes=[clo])
        dload(pbz[:], pb_pp_d, writes=[pbz])
        ve(lambda v: v.memset(ones16[:1, :], 1.0), writes=[ones16])
        ve(lambda v: v.memset(onesf[:1, :], 1.0), writes=[onesf])
        ve(lambda v: v.tensor_scalar(pbz[:], pbz[:], float(f32(1.0 / TWO_PI)),
                                     None, ALU.mult), reads=[pbz], writes=[pbz])

        with contextlib.ExitStack() as _es:
            abuf = _es.enter_context(nc.sbuf_tensor("abuf", [128, DC * 16], F32))
            rowbuf = _es.enter_context(nc.sbuf_tensor("rowbuf", [1, D], F32))
            bbuf = _es.enter_context(nc.sbuf_tensor("bbuf", [16, D], F32))
            for p, src in (("q", b_d["q"]), ("k", b_d["k"]), ("v", b_d["v"]),
                           ("f", fb_d)):
                dload(rowbuf[:1, :], src, writes=[rowbuf])
                h, l = brow[p]
                ve(lambda v, h=h: v.tensor_copy(h[:1, :], rowbuf[:1, :]),
                   reads=[rowbuf], writes=[h])
                ve(lambda v, h=h, l=l: v.scalar_tensor_tensor(
                    l[:1, :], h[:1, :], -1.0, rowbuf[:1, :], ALU.mult,
                    ALU.add),
                   reads=[rowbuf, h], writes=[l])
            for dc in range(DC):
                dload(abuf[:, dc * 16:(dc + 1) * 16],
                      a_d[dc * 128:(dc + 1) * 128, :], writes=[abuf])
            ve(lambda v: v.tensor_copy(a16h[:], abuf[:]), reads=[abuf],
               writes=[a16h])
            ve(lambda v: v.scalar_tensor_tensor(a16l[:], a16h[:], -1.0,
                                                abuf[:], ALU.mult, ALU.add),
               reads=[abuf, a16h], writes=[a16l])
            dload(bbuf[:16, :], bl_d, writes=[bbuf])
            ve(lambda v: v.tensor_copy(b16h[:16, :], bbuf[:16, :]),
               reads=[bbuf], writes=[b16h])
            ve(lambda v: v.scalar_tensor_tensor(b16l[:16, :], b16h[:16, :],
                                                -1.0, bbuf[:16, :], ALU.mult,
                                                ALU.add),
               reads=[bbuf, b16h], writes=[b16l])
            plan.barrier()
            plan.flush()

        with contextlib.ExitStack() as _es:
            kw = _es.enter_context(nc.sbuf_tensor("kw", [128, DC * 1024], F32))
            qs = _es.enter_context(nc.sbuf_tensor("qs", [128, 8], F32))
            qz = _es.enter_context(nc.sbuf_tensor("qz", [128, 1024], F32))
            qg = _es.enter_context(nc.sbuf_tensor("qg", [128, 1024], F32))
            qt = _es.enter_context(nc.sbuf_tensor("qt", [128, 1024], F32))
            k16a = _es.enter_context(nc.sbuf_tensor("k16a", [128, 1024], F16))
            k16b = _es.enter_context(nc.sbuf_tensor("k16b", [128, 1024], F16))
            qrow = _es.enter_context(nc.sbuf_tensor("qrow", [1, 128], F32))
            pq0 = _es.enter_context(nc.psum_tensor("pq0", [128, 512], F32))
            for dc in range(DC):
                dload(kw[:, dc * 1024:(dc + 1) * 1024],
                      kern_d[dc * 128:(dc + 1) * 128, :], writes=[kw])
            ve(lambda v: v.reduce_max(qs[:, 0:1], kw[:], axis=AX.X,
                                      apply_absolute_value=True),
               reads=[kw], writes=[qs])
            pe(lambda t: t.transpose(pq0[:1, 0:128], qs[:, 0:1], idf[:]),
               reads=[qs, idf], writes=[pq0])
            ve(lambda v: v.tensor_copy(qrow[:1, :], pq0[:1, 0:128]),
               reads=[pq0], writes=[qrow])
            ve(lambda v: v.reduce_max(qrow[:1, 0:1], qrow[:1, :], axis=AX.X),
               reads=[qrow], writes=[qrow])
            pe(lambda t: t.matmul(pq0[:, 0:1], onesf[:1, :], qrow[:1, 0:1],
                                  start=True, stop=True),
               reads=[qrow, onesf], writes=[pq0])
            ve(lambda v: v.tensor_copy(qs[:, 0:1], pq0[:, 0:1]),
               reads=[pq0], writes=[qs])
            c7h = float(f32(1.0 / 7.0))
            c7l = float(f32(np.float64(1.0 / 7.0) - np.float64(c7h)))
            ve(lambda v: v.tensor_scalar(qs[:, 1:2], qs[:, 0:1], c7h, None,
                                         ALU.mult), reads=[qs], writes=[qs])
            ve(lambda v: v.scalar_tensor_tensor(qs[:, 1:2], qs[:, 0:1], c7l,
                                                qs[:, 1:2], ALU.mult, ALU.add),
               reads=[qs], writes=[qs])
            ve(lambda v: v.reciprocal(qs[:, 2:3], qs[:, 0:1]),
               reads=[qs], writes=[qs])
            ve(lambda v: v.tensor_scalar(qs[:, 2:3], qs[:, 2:3], 7.0, None,
                                         ALU.mult), reads=[qs], writes=[qs])
            for dc in range(DC):
                sl = slice(dc * 1024, (dc + 1) * 1024)
                ve(lambda v, sl=sl: v.tensor_scalar(
                    qz[:], kw[:, sl], qs[:, 2:3], 8.5, ALU.mult, ALU.add),
                   reads=[kw, qs], writes=[qz])
                ve(lambda v: v.tensor_copy(qg[:], qz[:]), reads=[qz],
                   writes=[qg])
                for H in (8.0, 4.0, 2.0, 1.0):
                    ve(lambda v, H=H: v.tensor_scalar(
                        qt[:], qg[:], float(H), float(H), ALU.is_ge,
                        ALU.mult), reads=[qg], writes=[qt])
                    ve(lambda v: v.tensor_tensor(qg[:], qg[:], qt[:],
                                                 ALU.subtract),
                       reads=[qg, qt], writes=[qg])
                ve(lambda v: v.tensor_tensor(qz[:], qz[:], qg[:],
                                             ALU.subtract),
                   reads=[qz, qg], writes=[qz])
                ve(lambda v: v.tensor_scalar(qz[:], qz[:], 0.0, 15.0,
                                             ALU.max, ALU.min),
                   reads=[qz], writes=[qz])
                ve(lambda v: v.tensor_scalar(qz[:], qz[:], -8.0, qs[:, 1:2],
                                             ALU.add, ALU.mult),
                   reads=[qz, qs], writes=[qz])
                ve(lambda v: v.tensor_copy(k16a[:], qz[:]), reads=[qz],
                   writes=[k16a])
                ve(lambda v: v.scalar_tensor_tensor(k16b[:], k16a[:], -1.0,
                                                    qz[:], ALU.mult, ALU.add),
                   reads=[qz, k16a], writes=[k16b])
                dstore(kqh_d[dc * 128:(dc + 1) * 128, :], k16a[:],
                       reads=[k16a])
                dstore(kql_d[dc * 128:(dc + 1) * 128, :], k16b[:],
                       reads=[k16b])
            plan.barrier()
            plan.flush()

        with contextlib.ExitStack() as es2:
            wT = {p: (es2.enter_context(
                          nc.sbuf_tensor("wT" + p + "h", [128, DC * 1024], F16)),
                      es2.enter_context(
                          nc.sbuf_tensor("wT" + p + "l", [128, DC * 1024], F16)))
                  for p in "qkv"}
            with contextlib.ExitStack() as _es:
                wbuf = [_es.enter_context(
                    nc.sbuf_tensor("wbuf%d" % i, [128, 1024], F32))
                    for i in range(2)]
                wh16 = [_es.enter_context(
                    nc.sbuf_tensor("wh16%d" % i, [128, 1024], F16))
                    for i in range(2)]
                wl16 = [_es.enter_context(
                    nc.sbuf_tensor("wl16%d" % i, [128, 1024], F16))
                    for i in range(2)]
                ptr = [_es.enter_context(
                    nc.psum_tensor("ptr%d" % i, [128, 1024], F16))
                    for i in range(2)]
                it = 0
                for p in "qkv":
                    for ecw in range(DC):
                        wb, wh, wl = wbuf[it % 2], wh16[it % 2], wl16[it % 2]
                        dload(wb[:], w_d[p][ecw * 128:(ecw + 1) * 128, :],
                              writes=[wb])
                        ve(lambda v, wb=wb, wh=wh: v.tensor_copy(wh[:], wb[:]),
                           reads=[wb], writes=[wh])
                        ve(lambda v, wb=wb, wh=wh, wl=wl:
                           v.scalar_tensor_tensor(
                               wl[:], wh[:], -1.0, wb[:], ALU.mult, ALU.add),
                           reads=[wb, wh], writes=[wl])
                        for half, wsrc in ((0, wh), (1, wl)):
                            dstw = wT[p][half]
                            for dc in range(DC):
                                pt = ptr[dc % 2]
                                pe(lambda t, wsrc=wsrc, dc=dc, pt=pt:
                                   t.transpose(
                                       pt[:, 0:128],
                                       wsrc[:, dc * 128:(dc + 1) * 128],
                                       id16[:]),
                                   reads=[wsrc, id16], writes=[pt])
                                ve(lambda v, dstw=dstw, dc=dc, ecw=ecw, pt=pt:
                                   v.tensor_copy(
                                       dstw[:, dc * 1024 + ecw * 128:
                                            dc * 1024 + (ecw + 1) * 128],
                                       pt[:, 0:128]),
                                   reads=[pt], writes=[dstw])
                        it += 1
                plan.barrier()
                plan.flush()

            with contextlib.ExitStack() as _es:
                xt = [_es.enter_context(
                    nc.sbuf_tensor("xt%d" % i, [128, 4 * 1024], F32))
                    for i in range(2)]
                fmh = [_es.enter_context(
                    nc.sbuf_tensor("fmh%d" % i, [128, DC * 512], F16))
                    for i in range(2)]
                fml = [_es.enter_context(
                    nc.sbuf_tensor("fml%d" % i, [128, DC * 512], F16))
                    for i in range(2)]
                st1 = _es.enter_context(nc.sbuf_tensor("st1", [128, 8], F32))
                xc = _es.enter_context(nc.sbuf_tensor("xc", [128, 1024], F32))
                xsq = _es.enter_context(nc.sbuf_tensor("xsq", [128, 1024], F32))
                xnh = [_es.enter_context(
                    nc.sbuf_tensor("s_xnh%d" % i, [128, 1024], F16))
                    for i in range(2)]
                xnl = [_es.enter_context(
                    nc.sbuf_tensor("s_xnl%d" % i, [128, 1024], F16))
                    for i in range(2)]
                ptl = [_es.enter_context(
                    nc.psum_tensor("ptl%d" % i, [128, 1024], F16))
                    for i in range(2)]
                for sup in range(T // 512):
                    t0 = sup * 512
                    xb, fh, fl = xt[sup % 2], fmh[sup % 2], fml[sup % 2]
                    for i in range(4):
                        dload(xb[:, i * 1024:(i + 1) * 1024],
                              x_d[t0 + i * 128:t0 + (i + 1) * 128, :],
                              writes=[xb])
                    for i in range(4):
                        xh, xl = xnh[i % 2], xnl[i % 2]
                        xsl = xb[:, i * 1024:(i + 1) * 1024]
                        ve(lambda v, xsl=xsl, xb=xb: v.reduce_sum(
                            st1[:, 0:1], xsl, axis=AX.X),
                           reads=[xb], writes=[st1])
                        ve(lambda v: v.tensor_scalar(
                            st1[:, 1:2], st1[:, 0:1], float(f32(1.0 / D)),
                            None, ALU.mult), reads=[st1], writes=[st1])
                        ve(lambda v, xsl=xsl, xb=xb: v.tensor_scalar(
                            xc[:], xsl, st1[:, 1:2], None, ALU.subtract),
                           reads=[xb, st1], writes=[xc])
                        ve(lambda v: v.tensor_tensor(xsq[:], xc[:], xc[:],
                                                     ALU.mult),
                           reads=[xc], writes=[xsq])
                        ve(lambda v: v.reduce_sum(st1[:, 2:3], xsq[:],
                                                  axis=AX.X),
                           reads=[xsq], writes=[st1])
                        se(lambda s: s.activation(st1[:, 3:4], st1[:, 2:3],
                                                  ACT.Sqrt, bias=f32(1e-5),
                                                  scale=f32(1.0 / D)),
                           reads=[st1], writes=[st1])
                        ve(lambda v: v.reciprocal(st1[:, 4:5], st1[:, 3:4]),
                           reads=[st1], writes=[st1])
                        ve(lambda v: v.tensor_scalar(xc[:], xc[:],
                                                     st1[:, 4:5], None,
                                                     ALU.mult),
                           reads=[xc, st1], writes=[xc])
                        ve(lambda v, xh=xh: v.tensor_copy(xh[:], xc[:]),
                           reads=[xc], writes=[xh])
                        gp(lambda g, xh=xh, xl=xl: g.tensor_sub(
                            xl[:], xc[:], xh[:]),
                           reads=[xc, xh], writes=[xl])
                        for src16, fmdst in ((xh, fh), (xl, fl)):
                            for ec in range(DC):
                                pt = ptl[ec % 2]
                                pe(lambda t, src16=src16, ec=ec, pt=pt:
                                   t.transpose(
                                       pt[:, 0:128],
                                       src16[:, ec * 128:(ec + 1) * 128],
                                       id16[:]),
                                   reads=[src16, id16], writes=[pt])
                                ve(lambda v, fmdst=fmdst, ec=ec, i=i, pt=pt:
                                   v.tensor_copy(
                                       fmdst[:, ec * 512 + i * 128:
                                             ec * 512 + (i + 1) * 128],
                                       pt[:, 0:128]),
                                   reads=[pt], writes=[fmdst])
                    for ec in range(DC):
                        dstore(xnh_d[ec * 128:(ec + 1) * 128, t0:t0 + 512],
                               fh[:, ec * 512:(ec + 1) * 512], reads=[fh])
                        dstore(xnl_d[ec * 128:(ec + 1) * 128, t0:t0 + 512],
                               fl[:, ec * 512:(ec + 1) * 512], reads=[fl])
                plan.barrier()
                plan.flush()

            with contextlib.ExitStack() as _es:
                x16h = [_es.enter_context(
                    nc.sbuf_tensor("x16h%d" % i, [128, DC * 512], F16))
                    for i in range(2)]
                x16l = [_es.enter_context(
                    nc.sbuf_tensor("x16l%d" % i, [128, DC * 512], F16))
                    for i in range(2)]
                rr = [_es.enter_context(
                    nc.sbuf_tensor("rr%d" % i, [128, 1024], F32))
                    for i in range(3)]
                zc = _es.enter_context(nc.sbuf_tensor("zc", [128, 1024], F32))
                tg32 = [_es.enter_context(
                    nc.sbuf_tensor("tg32%d" % i, [128, 1024], F32))
                    for i in range(2)]
                tgh = [_es.enter_context(
                    nc.sbuf_tensor("tgh%d" % i, [128, 1024], F16))
                    for i in range(2)]
                tgl = [_es.enter_context(
                    nc.sbuf_tensor("tgl%d" % i, [128, 1024], F16))
                    for i in range(2)]
                vh = [_es.enter_context(
                    nc.sbuf_tensor("vh%d" % i, [128, 512], F16))
                    for i in range(2)]
                vl = [_es.enter_context(
                    nc.sbuf_tensor("vl%d" % i, [128, 512], F16))
                    for i in range(2)]
                vash = [_es.enter_context(
                    nc.sbuf_tensor("vash0", [128, 4 * 1024], F16))] * 2
                vasl = [_es.enter_context(
                    nc.sbuf_tensor("vasl0", [128, 4 * 1024], F16))] * 2
                pjs = [_es.enter_context(
                    nc.psum_tensor("pj%d" % i, [128, 512], F32))
                    for i in range(4)]
                ptv = [_es.enter_context(
                    nc.psum_tensor("ptv%d" % i, [128, 1024], F16))
                    for i in range(2)]

                def mm_proj_mk(ps, p, ec, xh, xl):
                    wh_t, wl_t = wT[p]
                    bh_t, bl_t = brow[p]

                    def mm_proj(t):
                        t.matmul(ps[:], bh_t[:1, ec * 128:(ec + 1) * 128],
                                 ones16[:1, 0:512], start=True, stop=False)
                        t.matmul(ps[:], bl_t[:1, ec * 128:(ec + 1) * 128],
                                 ones16[:1, 0:512], start=False, stop=False)
                        last = None
                        for dc in range(DC):
                            wsl = slice(dc * 1024 + ec * 128,
                                        dc * 1024 + (ec + 1) * 128)
                            xsl = slice(dc * 512, (dc + 1) * 512)
                            t.matmul(ps[:], wh_t[:, wsl], xh[:, xsl],
                                     start=False, stop=False)
                            t.matmul(ps[:], wh_t[:, wsl], xl[:, xsl],
                                     start=False, stop=False)
                            last = t.matmul(ps[:], wl_t[:, wsl], xh[:, xsl],
                                            start=False, stop=dc == DC - 1)
                        return last
                    return mm_proj, [wh_t, wl_t, bh_t, bl_t, xh, xl, ones16]

                trig_it = 0
                for blk in range(T // 512):
                    t0 = blk * 512
                    xh, xl = x16h[blk % 2], x16l[blk % 2]
                    for ec in range(DC):
                        dload(xh[:, ec * 512:(ec + 1) * 512],
                              xnh_d[ec * 128:(ec + 1) * 128, t0:t0 + 512],
                              writes=[xh])
                        dload(xl[:, ec * 512:(ec + 1) * 512],
                              xnl_d[ec * 128:(ec + 1) * 128, t0:t0 + 512],
                              writes=[xl])
                    vas_h, vas_l = vash[blk % 2], vasl[blk % 2]
                    for ec in range(DC):
                        psq = pjs[(3 * ec) % 4]
                        psk = pjs[(3 * ec + 1) % 4]
                        psv = pjs[(3 * ec + 2) % 4]
                        emq, rdq = mm_proj_mk(psq, "q", ec, xh, xl)
                        pe(emq, reads=rdq, writes=[psq])
                        emk, rdk = mm_proj_mk(psk, "k", ec, xh, xl)
                        pe(emk, reads=rdk, writes=[psk])
                        emv, rdv = mm_proj_mk(psv, "v", ec, xh, xl)
                        pe(emv, reads=rdv, writes=[psv])
                        rb = rr[ec % 3]
                        ve(lambda v, psq=psq, ec=ec, rb=rb: v.tensor_scalar(
                            rb[:, 0:512], psq[:], chi[:, ec:ec + 1],
                            pbz[:, ec:ec + 1], ALU.mult, ALU.add),
                           reads=[psq, chi, pbz], writes=[rb])
                        ve(lambda v, psq=psq, ec=ec, rb=rb:
                           v.scalar_tensor_tensor(
                               rb[:, 0:512], psq[:], clo[:, ec:ec + 1],
                               rb[:, 0:512], ALU.mult, ALU.add),
                           reads=[psq, clo, rb], writes=[rb])
                        ve(lambda v, psk=psk, ec=ec, rb=rb: v.tensor_scalar(
                            rb[:, 512:1024], psk[:], chi[:, ec:ec + 1],
                            pbz[:, ec:ec + 1], ALU.mult, ALU.add),
                           reads=[psk, chi, pbz], writes=[rb])
                        ve(lambda v, psk=psk, ec=ec, rb=rb:
                           v.scalar_tensor_tensor(
                               rb[:, 512:1024], psk[:], clo[:, ec:ec + 1],
                               rb[:, 512:1024], ALU.mult, ALU.add),
                           reads=[psk, clo, rb], writes=[rb])
                        vhb, vlb = vh[ec % 2], vl[ec % 2]
                        ve(lambda v, psv=psv, vhb=vhb: v.tensor_copy(
                            vhb[:], psv[:]), reads=[psv], writes=[vhb])
                        ve(lambda v, psv=psv, vhb=vhb, vlb=vlb:
                           v.scalar_tensor_tensor(
                               vlb[:], vhb[:], -1.0, psv[:], ALU.mult,
                               ALU.add), reads=[psv, vhb], writes=[vlb])
                        for vsrc, vdst in ((vhb, vas_h), (vlb, vas_l)):
                            for tc4 in range(4):
                                pt = ptv[tc4 % 2]
                                pe(lambda t, vsrc=vsrc, tc4=tc4, pt=pt:
                                   t.transpose(
                                       pt[:, 0:128],
                                       vsrc[:, tc4 * 128:(tc4 + 1) * 128],
                                       id16[:]),
                                   reads=[vsrc, id16], writes=[pt])
                                ve(lambda v, vdst=vdst, tc4=tc4, ec=ec, pt=pt:
                                   v.tensor_copy(
                                       vdst[:, tc4 * 1024 + ec * 128:
                                            tc4 * 1024 + (ec + 1) * 128],
                                       pt[:, 0:128]),
                                   reads=[pt], writes=[vdst])
                        m = M_FOLDS[ec]
                        for fn, shift in (("c", 0.0), ("s", 0.25)):
                            tg = tg32[trig_it % 2]
                            th_, tl_ = tgh[trig_it % 2], tgl[trig_it % 2]
                            nve = 4 if m >= 6 else 0
                            if nve:
                                ve(lambda v, rb=rb, shift=shift, tg=tg:
                                   v.tensor_scalar(
                                       tg[:], rb[:], float(shift), None,
                                       ALU.subtract) if shift else
                                   v.tensor_copy(tg[:], rb[:]),
                                   reads=[rb], writes=[tg])
                                ve(lambda v, tg=tg: v.scalar_tensor_tensor(
                                    tg[:], tg[:], -1.0, tg[:], ALU.mult,
                                    ALU.max), reads=[tg], writes=[tg])
                                for j in range(m - 1, m - 1 - nve, -1):
                                    ve(lambda v, j=j, tg=tg: v.tensor_scalar(
                                        tg[:], tg[:], float(2.0 ** j), None,
                                        ALU.subtract),
                                       reads=[tg], writes=[tg])
                                    ve(lambda v, tg=tg:
                                       v.scalar_tensor_tensor(
                                           tg[:], tg[:], -1.0, tg[:],
                                           ALU.mult, ALU.max),
                                       reads=[tg], writes=[tg])
                                se(lambda s, tg=tg: s.activation(
                                    zc[:], tg[:], ACT.Abs),
                                   reads=[tg], writes=[zc])
                            else:
                                se(lambda s, rb=rb, shift=shift: s.activation(
                                    zc[:], rb[:], ACT.Abs,
                                    bias=f32(-shift) if shift else f32(0.0)),
                                   reads=[rb], writes=[zc])
                            for j in range(m - 1 - nve, -1, -1):
                                se(lambda s, j=j: s.activation(
                                    zc[:], zc[:], ACT.Abs,
                                    bias=f32(-(2.0 ** j))),
                                   reads=[zc], writes=[zc])
                            se(lambda s: s.activation(zc[:], zc[:], ACT.Abs,
                                                      bias=f32(-0.5)),
                               reads=[zc], writes=[zc])
                            se(lambda s, tg=tg: s.activation(
                                tg[:], zc[:], ACT.Sin, bias=f32(-PI / 2),
                                scale=f32(TWO_PI)),
                               reads=[zc], writes=[tg])
                            gp(lambda g, tg=tg, th_=th_: g.tensor_copy(
                                th_[:], tg[:]), reads=[tg], writes=[th_])
                            gp(lambda g, tg=tg, th_=th_, tl_=tl_:
                               g.tensor_sub(tl_[:], tg[:], th_[:]),
                               reads=[tg, th_], writes=[tl_])
                            qnm = "cq" if fn == "c" else "sq"
                            knm = "ck" if fn == "c" else "sk"
                            esl = slice(ec * 128, (ec + 1) * 128)
                            dstore(feat_d[qnm][0][esl, t0:t0 + 512],
                                   th_[:, 0:512], reads=[th_])
                            dstore(feat_d[knm][0][esl, t0:t0 + 512],
                                   th_[:, 512:1024], reads=[th_])
                            dstore(feat_d[qnm][1][esl, t0:t0 + 512],
                                   tl_[:, 0:512], reads=[tl_])
                            dstore(feat_d[knm][1][esl, t0:t0 + 512],
                                   tl_[:, 512:1024], reads=[tl_])
                            trig_it += 1
                    for tc4 in range(4):
                        dstore(vth_d[t0 + tc4 * 128:t0 + (tc4 + 1) * 128, :],
                               vas_h[:, tc4 * 1024:(tc4 + 1) * 1024],
                               reads=[vas_h])
                        dstore(vtl_d[t0 + tc4 * 128:t0 + (tc4 + 1) * 128, :],
                               vas_l[:, tc4 * 1024:(tc4 + 1) * 1024],
                               reads=[vas_l])
                plan.barrier()
                plan.flush()

        with contextlib.ExitStack() as _es:
            cqh_s = _es.enter_context(nc.sbuf_tensor("cqh_s", [128, DC * QQ], F16))
            cql_s = _es.enter_context(nc.sbuf_tensor("cql_s", [128, DC * QQ], F16))
            sqh_s = _es.enter_context(nc.sbuf_tensor("sqh_s", [128, DC * QQ], F16))
            sql_s = _es.enter_context(nc.sbuf_tensor("sql_s", [128, DC * QQ], F16))
            ckh2 = _es.enter_context(nc.sbuf_tensor("ckh_s", [128, DC * TB], F16))
            ckl2 = _es.enter_context(nc.sbuf_tensor("ckl_s", [128, DC * TB], F16))
            skh2 = _es.enter_context(nc.sbuf_tensor("skh_s", [128, DC * TB], F16))
            skl2 = _es.enter_context(nc.sbuf_tensor("skl_s", [128, DC * TB], F16))
            simr = _es.enter_context(
                nc.sbuf_tensor("simr", [128, 4 * 2048], F32))
            prob = [_es.enter_context(
                nc.sbuf_tensor("prob%d" % i, [128, 2048], F32))
                for i in range(2)]
            ph1 = [_es.enter_context(
                nc.sbuf_tensor("ph1%d" % i, [128, 2048], F16))
                for i in range(2)]
            pl1 = [_es.enter_context(
                nc.sbuf_tensor("pl1%d" % i, [128, 2048], F16))
                for i in range(2)]
            ptTh = _es.enter_context(
                nc.sbuf_tensor("ptTh", [128, NSC * QQ], F16))
            ptTl = _es.enter_context(
                nc.sbuf_tensor("ptTl", [128, NSC * QQ], F16))
            vb_h = _es.enter_context(nc.sbuf_tensor("vbh", [128, TC * 1024], F16))
            vb_l = _es.enter_context(nc.sbuf_tensor("vbl", [128, TC * 1024], F16))
            smax = _es.enter_context(nc.sbuf_tensor("smax", [128, 8], F32))
            aout = [_es.enter_context(
                nc.sbuf_tensor("aout0", [128, 1024], F32))] * 2
            pas = [_es.enter_context(
                nc.psum_tensor("pa%d" % i, [128, 512], F32))
                for i in range(4)]
            ptp = [_es.enter_context(
                nc.psum_tensor("ptp%d" % i, [128, 1024], F16))
                for i in range(2)]
            assert SB <= 4 and S <= 2048
            qarr = {"cq": (cqh_s, cql_s), "sq": (sqh_s, sql_s)}
            for seq in range(SEQS):
                sbase = seq * S
                for qq in range(NQQ):
                    q0 = sbase + qq * QQ
                    for nm in ("cq", "sq"):
                        bh2, bl2 = qarr[nm]
                        sh_d, sl_d = feat_d[nm]
                        for ec in range(DC):
                            dload(bh2[:, ec * QQ:(ec + 1) * QQ],
                                  sh_d[ec * 128:(ec + 1) * 128, q0:q0 + QQ],
                                  writes=[bh2])
                            dload(bl2[:, ec * QQ:(ec + 1) * QQ],
                                  sl_d[ec * 128:(ec + 1) * 128, q0:q0 + QQ],
                                  writes=[bl2])
                    for tb in range(NTB):
                        k0 = sbase + tb * TB
                        for ec in range(DC):
                            dload(ckh2[:, ec * TB:(ec + 1) * TB],
                                  feat_d["ck"][0][ec * 128:(ec + 1) * 128,
                                                  k0:k0 + TB], writes=[ckh2])
                            dload(ckl2[:, ec * TB:(ec + 1) * TB],
                                  feat_d["ck"][1][ec * 128:(ec + 1) * 128,
                                                  k0:k0 + TB], writes=[ckl2])
                        for sbi in range(SB):
                            ps = pas[sbi % 4]

                            def mm_cos(t, ps=ps, sbi=sbi):
                                qh2, ql2 = qarr["cq"]
                                last = None
                                for ec in range(DC):
                                    qsl = slice(ec * QQ + sbi * 128,
                                                ec * QQ + (sbi + 1) * 128)
                                    ksl = slice(ec * TB, (ec + 1) * TB)
                                    t.matmul(ps[:, 0:TB], qh2[:, qsl],
                                             ckh2[:, ksl], start=ec == 0,
                                             stop=False)
                                    t.matmul(ps[:, 0:TB], qh2[:, qsl],
                                             ckl2[:, ksl], start=False,
                                             stop=False)
                                    last = t.matmul(ps[:, 0:TB], ql2[:, qsl],
                                                    ckh2[:, ksl],
                                                    start=False, stop=False)
                                return last

                            pe(mm_cos, reads=[cqh_s, cql_s, ckh2, ckl2],
                               writes=[ps])
                        for ec in range(DC):
                            dload(skh2[:, ec * TB:(ec + 1) * TB],
                                  feat_d["sk"][0][ec * 128:(ec + 1) * 128,
                                                  k0:k0 + TB], writes=[skh2])
                            dload(skl2[:, ec * TB:(ec + 1) * TB],
                                  feat_d["sk"][1][ec * 128:(ec + 1) * 128,
                                                  k0:k0 + TB], writes=[skl2])
                        for sbi in range(SB):
                            ps = pas[sbi % 4]

                            def mm_sin(t, ps=ps, sbi=sbi):
                                qh2, ql2 = qarr["sq"]
                                last = None
                                for ec in range(DC):
                                    qsl = slice(ec * QQ + sbi * 128,
                                                ec * QQ + (sbi + 1) * 128)
                                    ksl = slice(ec * TB, (ec + 1) * TB)
                                    t.matmul(ps[:, 0:TB], qh2[:, qsl],
                                             skh2[:, ksl], start=False,
                                             stop=False)
                                    t.matmul(ps[:, 0:TB], qh2[:, qsl],
                                             skl2[:, ksl], start=False,
                                             stop=False)
                                    last = t.matmul(ps[:, 0:TB], ql2[:, qsl],
                                                    skh2[:, ksl],
                                                    start=False,
                                                    stop=ec == DC - 1)
                                return last

                            pe(mm_sin, reads=[sqh_s, sql_s, skh2, skl2],
                               writes=[ps])
                            se(lambda s, ps=ps, sbi=sbi, tb=tb: s.activation(
                                simr[:, sbi * S + tb * TB:
                                     sbi * S + (tb + 1) * TB],
                                ps[:, 0:TB], ACT.Copy),
                               reads=[ps], writes=[simr])
                            ve(lambda v, sbi=sbi, tb=tb: v.reduce_max(
                                smax[:, 6:7],
                                simr[:, sbi * S + tb * TB:
                                     sbi * S + (tb + 1) * TB], axis=AX.X),
                               reads=[simr], writes=[smax])
                            if tb == 0:
                                ve(lambda v, sbi=sbi: v.tensor_copy(
                                    smax[:, sbi:sbi + 1], smax[:, 6:7]),
                                   reads=[smax], writes=[smax])
                            else:
                                ve(lambda v, sbi=sbi: v.tensor_tensor(
                                    smax[:, sbi:sbi + 1],
                                    smax[:, sbi:sbi + 1], smax[:, 6:7],
                                    ALU.max), reads=[smax], writes=[smax])
                    for sbi in range(SB):
                        ssl = slice(sbi * S, (sbi + 1) * S)
                        pr, phb, plb = (prob[sbi % 2], ph1[sbi % 2],
                                        pl1[sbi % 2])
                        ve(lambda v, sbi=sbi: v.tensor_scalar(
                            smax[:, 4 + sbi:5 + sbi], smax[:, sbi:sbi + 1],
                            float(f32(-1.0 / 32.0)), None, ALU.mult),
                           reads=[smax], writes=[smax])
                        se(lambda s, ssl=ssl, sbi=sbi, pr=pr: s.activation(
                            pr[:, 0:S], simr[:, ssl], ACT.Exp,
                            bias=smax[:, 4 + sbi:5 + sbi],
                            scale=f32(1.0 / 32.0)),
                           reads=[simr, smax], writes=[pr])
                        ve(lambda v, sbi=sbi, pr=pr: v.reduce_sum(
                            smax[:, sbi:sbi + 1], pr[:, 0:S], axis=AX.X),
                           reads=[pr], writes=[smax])
                        ve(lambda v, sbi=sbi: v.reciprocal(
                            smax[:, 4 + sbi:5 + sbi], smax[:, sbi:sbi + 1]),
                           reads=[smax], writes=[smax])
                        gp(lambda g, pr=pr, phb=phb: g.tensor_copy(
                            phb[:, 0:S], pr[:, 0:S]),
                           reads=[pr], writes=[phb])
                        gp(lambda g, pr=pr, phb=phb, plb=plb: g.tensor_sub(
                            plb[:, 0:S], pr[:, 0:S], phb[:, 0:S]),
                           reads=[pr, phb], writes=[plb])
                        for tcg in range(NSC):
                            for src1, dstT in ((phb, ptTh), (plb, ptTl)):
                                pt = ptp[tcg % 2]
                                pe(lambda t, src1=src1, tcg=tcg, pt=pt:
                                   t.transpose(
                                       pt[:, 0:128],
                                       src1[:, tcg * 128:(tcg + 1) * 128],
                                       id16[:]),
                                   reads=[src1, id16], writes=[pt])
                                if tcg % 2 == 0:
                                    ve(lambda v, dstT=dstT, tcg=tcg, sbi=sbi,
                                       pt=pt: v.tensor_copy(
                                           dstT[:, tcg * QQ + sbi * 128:
                                                tcg * QQ + (sbi + 1) * 128],
                                           pt[:, 0:128]),
                                       reads=[pt], writes=[dstT])
                                else:
                                    se(lambda s, dstT=dstT, tcg=tcg, sbi=sbi,
                                       pt=pt: s.activation(
                                           dstT[:, tcg * QQ + sbi * 128:
                                                tcg * QQ + (sbi + 1) * 128],
                                           pt[:, 0:128], ACT.Copy),
                                       reads=[pt], writes=[dstT])
                    for pair in range(SB // 2):
                        sb0 = pair * 2
                        for tb in range(NTB):
                            k0r = sbase + tb * TB
                            for tci in range(TC):
                                dload(vb_h[:, tci * 1024:(tci + 1) * 1024],
                                      vth_d[k0r + tci * 128:
                                            k0r + (tci + 1) * 128, :],
                                      writes=[vb_h])
                            for sbi in range(sb0, sb0 + 2):
                                for dh in range(2):
                                    pv = pas[(sbi - sb0) * 2 + dh]
                                    strt = tb == 0

                                    def mm_pvh(t, pv=pv, sbi=sbi, dh=dh,
                                               tb=tb, strt=strt):
                                        last = None
                                        for tci in range(TC):
                                            tcg = tb * TC + tci
                                            tsl = slice(
                                                tcg * QQ + sbi * 128,
                                                tcg * QQ + (sbi + 1) * 128)
                                            vsl = slice(
                                                tci * 1024 + dh * 512,
                                                tci * 1024 + (dh + 1) * 512)
                                            t.matmul(pv[:], ptTh[:, tsl],
                                                     vb_h[:, vsl],
                                                     start=strt and tci == 0,
                                                     stop=False)
                                            last = t.matmul(
                                                pv[:], ptTl[:, tsl],
                                                vb_h[:, vsl], start=False,
                                                stop=False)
                                        return last

                                    pe(mm_pvh, reads=[ptTh, ptTl, vb_h],
                                       writes=[pv])
                            for tci in range(TC):
                                dload(vb_l[:, tci * 1024:(tci + 1) * 1024],
                                      vtl_d[k0r + tci * 128:
                                            k0r + (tci + 1) * 128, :],
                                      writes=[vb_l])
                            for sbi in range(sb0, sb0 + 2):
                                for dh in range(2):
                                    pv = pas[(sbi - sb0) * 2 + dh]
                                    stp = tb == NTB - 1

                                    def mm_pvl(t, pv=pv, sbi=sbi, dh=dh,
                                               tb=tb, stp=stp):
                                        last = None
                                        for tci in range(TC):
                                            tcg = tb * TC + tci
                                            tsl = slice(
                                                tcg * QQ + sbi * 128,
                                                tcg * QQ + (sbi + 1) * 128)
                                            vsl = slice(
                                                tci * 1024 + dh * 512,
                                                tci * 1024 + (dh + 1) * 512)
                                            last = t.matmul(
                                                pv[:], ptTh[:, tsl],
                                                vb_l[:, vsl], start=False,
                                                stop=(stp and
                                                      tci == TC - 1))
                                        return last

                                    pe(mm_pvl, reads=[ptTh, vb_l],
                                       writes=[pv])
                        for sbi in range(sb0, sb0 + 2):
                            ao = aout[sbi % 2]
                            for dh in range(2):
                                pv = pas[(sbi - sb0) * 2 + dh]
                                ve(lambda v, pv=pv, dh=dh, sbi=sbi, ao=ao:
                                   v.tensor_scalar(
                                       ao[:, dh * 512:(dh + 1) * 512],
                                       pv[:], smax[:, 4 + sbi:5 + sbi], None,
                                       ALU.mult),
                                   reads=[pv, smax], writes=[ao])
                            dstore(attn_d[q0 + sbi * 128:
                                          q0 + (sbi + 1) * 128, :],
                                   ao[:], reads=[ao])
            plan.barrier()
            plan.flush()

        with contextlib.ExitStack() as _es:
            k16h_s = _es.enter_context(
                nc.sbuf_tensor("k16h_s", [128, DC * 1024], F16))
            k16l_s = _es.enter_context(
                nc.sbuf_tensor("k16l_s", [128, DC * 1024], F16))
            x16h4 = [_es.enter_context(
                nc.sbuf_tensor("x16h4%d" % i, [128, DC * 512], F16))
                for i in range(2)]
            x16l4 = [_es.enter_context(
                nc.sbuf_tensor("x16l4%d" % i, [128, DC * 512], F16))
                for i in range(2)]
            xres = [_es.enter_context(
                nc.sbuf_tensor("xres%d" % i, [128, 4 * 1024], F32))
                for i in range(2)]
            ares = [_es.enter_context(
                nc.sbuf_tensor("ares%d" % i, [128, 4 * 1024], F32))
                for i in range(2)]
            t1h = [_es.enter_context(
                nc.sbuf_tensor("t1h%d" % i, [16, 512], F16))
                for i in range(2)]
            t1l = [_es.enter_context(
                nc.sbuf_tensor("t1l%d" % i, [16, 512], F16))
                for i in range(2)]
            oacc = [_es.enter_context(
                nc.sbuf_tensor("oacc%d" % i, [128, 1024], F32))
                for i in range(2)]
            pfs = [_es.enter_context(
                nc.psum_tensor("pf%d" % i, [128, 512], F32))
                for i in range(4)]
            pt1 = [_es.enter_context(
                nc.psum_tensor("pt1%d" % i, [128, 512], F32))
                for i in range(2)]
            for dc in range(DC):
                dload(k16h_s[:, dc * 1024:(dc + 1) * 1024],
                      kqh_d[dc * 128:(dc + 1) * 128, :], writes=[k16h_s])
                dload(k16l_s[:, dc * 1024:(dc + 1) * 1024],
                      kql_d[dc * 128:(dc + 1) * 128, :], writes=[k16l_s])
            for blk in range(T // 512):
                t0 = blk * 512
                xh4, xl4 = x16h4[blk % 2], x16l4[blk % 2]
                xr, ar = xres[blk % 2], ares[blk % 2]
                for ec in range(DC):
                    dload(xh4[:, ec * 512:(ec + 1) * 512],
                          xnh_d[ec * 128:(ec + 1) * 128, t0:t0 + 512],
                          writes=[xh4])
                    dload(xl4[:, ec * 512:(ec + 1) * 512],
                          xnl_d[ec * 128:(ec + 1) * 128, t0:t0 + 512],
                          writes=[xl4])
                for i in range(4):
                    dload(xr[:, i * 1024:(i + 1) * 1024],
                          x_d[t0 + i * 128:t0 + (i + 1) * 128, :],
                          writes=[xr])
                    dload(ar[:, i * 1024:(i + 1) * 1024],
                          attn_d[t0 + i * 128:t0 + (i + 1) * 128, :],
                          writes=[ar])
                th1, tl1 = t1h[blk % 2], t1l[blk % 2]
                p1 = pt1[blk % 2]

                def mm_t1(t, xh4=xh4, xl4=xl4, p1=p1):
                    last = None
                    for dc in range(DC):
                        asl = slice(dc * 16, (dc + 1) * 16)
                        xsl = slice(dc * 512, (dc + 1) * 512)
                        t.matmul(p1[:16, :], a16h[:, asl], xh4[:, xsl],
                                 start=dc == 0, stop=False)
                        t.matmul(p1[:16, :], a16h[:, asl], xl4[:, xsl],
                                 start=False, stop=False)
                        last = t.matmul(p1[:16, :], a16l[:, asl],
                                        xh4[:, xsl], start=False,
                                        stop=dc == DC - 1)
                    return last

                pe(mm_t1, reads=[a16h, a16l, xh4, xl4], writes=[p1])
                ve(lambda v, th1=th1, p1=p1: v.tensor_copy(th1[:16, :],
                                                           p1[:16, :]),
                   reads=[p1], writes=[th1])
                ve(lambda v, th1=th1, tl1=tl1, p1=p1: v.scalar_tensor_tensor(
                    tl1[:16, :], th1[:16, :], -1.0, p1[:16, :], ALU.mult,
                    ALU.add), reads=[p1, th1], writes=[tl1])
                for i in range(4):
                    oa = oacc[i % 2]
                    for dh in range(2):
                        pf = pfs[(i % 2) * 2 + dh]
                        fsl = slice(dh * 512, (dh + 1) * 512)

                        def mm_ffn(t, pf=pf, fsl=fsl, i=i, xh4=xh4, xl4=xl4,
                                   th1=th1, tl1=tl1):
                            fh, fl = brow["f"]
                            t.matmul(pf[:], ones16[:1, 0:128], fh[:1, fsl],
                                     start=True, stop=False)
                            t.matmul(pf[:], ones16[:1, 0:128], fl[:1, fsl],
                                     start=False, stop=False)
                            for dc in range(DC):
                                ksl = slice(dc * 1024 + fsl.start,
                                            dc * 1024 + fsl.stop)
                                xsl = slice(dc * 512 + i * 128,
                                            dc * 512 + (i + 1) * 128)
                                t.matmul(pf[:], xh4[:, xsl], k16h_s[:, ksl],
                                         start=False, stop=False)
                                t.matmul(pf[:], xh4[:, xsl], k16l_s[:, ksl],
                                         start=False, stop=False)
                                t.matmul(pf[:], xl4[:, xsl], k16h_s[:, ksl],
                                         start=False, stop=False)
                            tsl = slice(i * 128, (i + 1) * 128)
                            t.matmul(pf[:], th1[:16, tsl], b16h[:16, fsl],
                                     start=False, stop=False)
                            t.matmul(pf[:], th1[:16, tsl], b16l[:16, fsl],
                                     start=False, stop=False)
                            return t.matmul(pf[:], tl1[:16, tsl],
                                            b16h[:16, fsl], start=False,
                                            stop=True)

                        pe(mm_ffn, reads=[xh4, xl4, k16h_s, k16l_s, th1,
                                          tl1, b16h, b16l, ones16,
                                          brow["f"][0], brow["f"][1]],
                           writes=[pf])
                    ve(lambda v, i=i, oa=oa, xr=xr, ar=ar: v.tensor_tensor(
                        oa[:], xr[:, i * 1024:(i + 1) * 1024],
                        ar[:, i * 1024:(i + 1) * 1024], ALU.add),
                       reads=[xr, ar], writes=[oa])
                    ve(lambda v, oa=oa, pf=pfs[(i % 2) * 2]: v.tensor_tensor(
                        oa[:, 0:512], oa[:, 0:512], pf[:], ALU.add),
                       reads=[pfs[(i % 2) * 2], oa], writes=[oa])
                    ve(lambda v, oa=oa, pf=pfs[(i % 2) * 2 + 1]:
                       v.tensor_tensor(
                           oa[:, 512:1024], oa[:, 512:1024], pf[:], ALU.add),
                       reads=[pfs[(i % 2) * 2 + 1], oa], writes=[oa])
                    dstore(out_d[t0 + i * 128:t0 + (i + 1) * 128, :],
                           oa[:], reads=[oa])
            plan.barrier()
            plan.flush()
    return nc


_NC_CACHE = {}


def _get_nc(key):
    if key not in _NC_CACHE:
        _NC_CACHE[key] = build_nc(*key)
    return _NC_CACHE[key]


def make_in_maps(L0_attn_x, L0_attn_Wq, L0_attn_bq, L0_attn_Wk, L0_attn_bk,
                 L0_attn_Wv, L0_attn_bv, L0_attn_phase_bias, L0_ffn_kernel,
                 L0_ffn_bias, L0_ffn_a, L0_ffn_b, n_cores=N_CORES):
    x = np.ascontiguousarray(np.asarray(L0_attn_x, dtype=np.float32))
    B, S, _ = x.shape
    SEQS = B // n_cores
    common = {
        "wq": np.ascontiguousarray(np.asarray(L0_attn_Wq, np.float32)),
        "wk": np.ascontiguousarray(np.asarray(L0_attn_Wk, np.float32)),
        "wv": np.ascontiguousarray(np.asarray(L0_attn_Wv, np.float32)),
        "bq": np.asarray(L0_attn_bq, np.float32).reshape(1, D),
        "bk": np.asarray(L0_attn_bk, np.float32).reshape(1, D),
        "bv": np.asarray(L0_attn_bv, np.float32).reshape(1, D),
        "pb_pp": _pp(np.asarray(L0_attn_phase_bias, np.float32)),
        "kern": np.ascontiguousarray(np.asarray(L0_ffn_kernel, np.float32)),
        "fb": np.asarray(L0_ffn_bias, np.float32).reshape(1, D),
        "a": np.ascontiguousarray(np.asarray(L0_ffn_a, np.float32)),
        "bl": np.ascontiguousarray(np.asarray(L0_ffn_b, np.float32)),
        "chi": _pp(C2PI_HI),
        "clo": _pp(C2PI_LO),
        "id16": np.eye(128, dtype=np.float16),
        "idf": np.eye(128, dtype=np.float32),
    }
    in_maps = []
    for c in range(n_cores):
        m = dict(common)
        m["x"] = np.ascontiguousarray(
            x[c * SEQS:(c + 1) * SEQS].reshape(SEQS * S, D))
        in_maps.append(m)
    return in_maps, SEQS, S


def kernel(**inputs):
    in_maps, SEQS, S = make_in_maps(**inputs)
    nc = _get_nc((SEQS, S, 512, 512))
    res = run_bass_kernel_spmd(nc, in_maps, core_ids=list(range(N_CORES)))
    outs = [np.asarray(r["out"]).reshape(SEQS, S, D) for r in res.results]
    return np.concatenate(outs, axis=0)
